# version 01
# baseline (speedup 1.0000x reference)
import numpy as np
import ml_dtypes
from contextlib import ExitStack
import concourse.bass as bass
import concourse.mybir as mybir
from concourse.bass_utils import run_bass_kernel_spmd
from concourse.bass_core_rust import IndirectOffsetOnAxis

dt = mybir.dt
AF = mybir.ActivationFunctionType
ALU = mybir.AluOpType
AX = mybir.AxisListType.X
F32 = dt.float32
BF16 = dt.bfloat16
U32 = dt.uint32

NCORES = 8
D = 2048
TPC = 4096
DCH = 16

BF16NP = ml_dtypes.bfloat16


class Prog:

    def __init__(self):
        self.ops = []

    def _add(self, eng, fn, inc=1):
        self.ops.append((eng, fn, inc))

    def dma(self, out, in_, q='sync'):
        self._add(q, lambda e: e.dma_start(out=out, in_=in_), 16)

    def idma_gather(self, out, in_, idx_ap, bc):
        def f(e):
            r = self._bc_reg(e, bc)
            return e.indirect_dma_start(
                out=out, out_offset=None, in_=in_,
                in_offset=IndirectOffsetOnAxis(ap=idx_ap, axis=0),
                bounds_check=r, oob_is_err=False)
        self._add('gpsimd', f, 16)

    def idma_scatter(self, out, in_, idx_ap, bc):
        def f(e):
            r = self._bc_reg(e, bc)
            return e.indirect_dma_start(
                out=out, out_offset=IndirectOffsetOnAxis(ap=idx_ap, axis=0),
                in_=in_, in_offset=None,
                bounds_check=r, oob_is_err=False)
        self._add('gpsimd', f, 16)

    def _bc_reg(self, e, bc):
        if not hasattr(self, '_bcregs'):
            self._bcregs = {}
        if bc not in self._bcregs:
            self._bcregs[bc] = e.to_reg(bc)
        return self._bcregs[bc]

    def mm(self, out, lhsT, rhs, start, stop):
        self._add('tensor', lambda e: e.matmul(out=out, lhsT=lhsT, rhs=rhs,
                                               start=start, stop=stop))

    def tr(self, out, in_, ident):
        self._add('tensor', lambda e: e.transpose(out=out, in_=in_, identity=ident))

    def act(self, out, in_, func, bias=None, scale=None):
        kw = {}
        if bias is not None:
            kw['bias'] = bias
        if scale is not None:
            kw['scale'] = scale
        self._add('scalar', lambda e: e.activation(out=out, in_=in_, func=func, **kw))

    def scopy(self, out, in_):
        self._add('scalar', lambda e: e.copy(out=out, in_=in_))

    def copy(self, out, in_):
        self._add('vector', lambda e: e.tensor_copy(out=out, in_=in_))

    def tt(self, out, in0, in1, op):
        self._add('vector', lambda e: e.tensor_tensor(out=out, in0=in0, in1=in1, op=op))

    def ts(self, out, in0, s1, op0, s2=0.0, op1=ALU.add):
        self._add('vector', lambda e: e.tensor_scalar(out=out, in0=in0, scalar1=s1,
                                                      scalar2=s2, op0=op0, op1=op1))

    def stt(self, out, in0, s, in1, op0, op1):
        self._add('vector', lambda e: e.scalar_tensor_tensor(
            out=out, in0=in0, scalar=s, in1=in1, op0=op0, op1=op1))

    def rsum(self, out, in_):
        self._add('vector', lambda e: e.reduce_sum(out=out, in_=in_, axis=AX))

    def rmax(self, out, in_):
        self._add('vector', lambda e: e.reduce_max(out=out, in_=in_, axis=AX))

    def max8(self, out, in_):
        self._add('vector', lambda e: e.max(out=out, in_=in_))

    def mrep(self, out, vals8, in_values, imm=0.0):
        self._add('vector', lambda e: e.match_replace(out=out, in_to_replace=vals8,
                                                      in_values=in_values, imm_value=imm))

    def recip(self, out, in_):
        self._add('vector', lambda e: e.reciprocal(out=out, in_=in_))

    def memset(self, ap, v):
        self._add('vector', lambda e: e.memset(ap, v))

    def emit(self, nc, blk, S):
        totals = []
        t = 0
        for (_, _, inc) in self.ops:
            totals.append(t)
            t += inc
        final = t
        engs = ['sync', 'tensor', 'vector', 'scalar', 'gpsimd']

        def mk(section):
            def body(eng):
                for (en, fn, inc), before in zip(self.ops, totals):
                    if en != section:
                        continue
                    eng.wait_ge(S, before)
                    fn(eng).then_inc(S, inc)
                if section == 'sync':
                    eng.wait_ge(S, final)
            return body

        for en in engs:
            getattr(blk, en)(mk(en))



def build_l0ffn():
    nc = bass.Bass("TRN2", target_bir_lowering=False)
    H = 5120
    YIN = nc.dram_tensor("yin", [D, TPC], F32, kind="ExternalInput").ap()
    WFC = nc.dram_tensor("wfc", [D, H], F32, kind="ExternalInput").ap()
    WPR = nc.dram_tensor("wpr", [H, D], F32, kind="ExternalInput").ap()
    CONST = nc.dram_tensor("const", [128, 128], F32, kind="ExternalInput").ap()
    YOUT = nc.dram_tensor("yout", [D, TPC], F32, kind="ExternalOutput").ap()
    XN = nc.dram_tensor("xn_scr", [D, TPC], BF16, kind="Internal").ap()
    HH = nc.dram_tensor("hh_scr", [H, TPC], BF16, kind="Internal").ap()

    p = Prog()
    with ExitStack() as _st:
        ec = _st.enter_context
        ones = ec(nc.sbuf_tensor("ones", [128, 128], F32))
        wbuf = ec(nc.sbuf_tensor("wbuf", [128, 32768], BF16))
        lbuf = ec(nc.sbuf_tensor("lbuf", [128, 2560], F32))
        s2 = ec(nc.sbuf_tensor("s2", [128, 2560], F32))
        s3 = ec(nc.sbuf_tensor("s3", [128, 2560], F32))
        xt = ec(nc.sbuf_tensor("xt", [128, 8192], F32))
        xnb = ec(nc.sbuf_tensor("xnb", [128, 8192], BF16))
        hhb = ec(nc.sbuf_tensor("hhb", [128, 10240], BF16))
        sq = ec(nc.sbuf_tensor("sq", [128, 512], F32))
        obuf = ec(nc.sbuf_tensor("obuf", [128, 512], F32))
        rows = ec(nc.sbuf_tensor("rows", [1, 4096], F32))
        rows2 = ec(nc.sbuf_tensor("rows2", [1, 4096], F32))
        cola = ec(nc.sbuf_tensor("cola", [128, 8], F32))
        P0 = ec(nc.psum_tensor("P0", [128, 512], F32))
        P2 = ec(nc.psum_tensor("P2", [128, 512], F32))
        R0 = ec(nc.psum_tensor("R0", [1, 512], F32))
        R1 = ec(nc.psum_tensor("R1", [1, 512], F32))
        S = ec(nc.semaphore("S"))
        blk = ec(nc.Block())
        ones_col = ones[:, 0:1]
        ones_row = ones[0:1, :]

        p.dma(ones[:, :], CONST[:, :])

        for ti in range(8):
            t0 = ti * 512
            tw = 512
            for c in range(DCH):
                p.dma(xt[:, c * tw:(c + 1) * tw], YIN[c * 128:(c + 1) * 128, t0:t0 + tw])
            for c in range(DCH):
                cs = slice(c * tw, (c + 1) * tw)
                p.mm(R0[0:1, 0:tw], ones_col, xt[:, cs], start=(c == 0), stop=(c == DCH - 1))
                p.act(sq[:, 0:tw], xt[:, cs], AF.Square)
                p.mm(R1[0:1, 0:tw], ones_col, sq[:, 0:tw], start=(c == 0), stop=(c == DCH - 1))
            m = rows[0:1, 0:tw]
            v = rows2[0:1, 0:tw]
            p.ts(m, R0[0:1, 0:tw], 1.0 / 2048.0, ALU.mult)
            p.ts(v, R1[0:1, 0:tw], 1.0 / 2048.0, ALU.mult)
            mm_r = rows[0:1, 512:512 + tw]
            p.tt(mm_r, m, m, ALU.mult)
            p.tt(v, v, mm_r, ALU.subtract)
            p.ts(v, v, 1e-5, ALU.add)
            p.act(v, v, AF.Ln)
            p.act(v, v, AF.Exp, scale=-0.5)
            p.tt(mm_r, m, v, ALU.mult)
            p.mm(P0[:, 0:tw], ones_row, v, start=True, stop=True)
            p.mm(P2[:, 0:tw], ones_row, mm_r, start=True, stop=True)
            rstd_b = xt
            p.scopy(sq[:, 0:tw], P0[:, 0:tw])
            p.scopy(obuf[:, 0:tw], P2[:, 0:tw])
            for c in range(DCH):
                cs = slice(c * tw, (c + 1) * tw)
                p.stt(xnb[:, cs], xt[:, cs], 0.0, sq[:, 0:tw], ALU.add, ALU.mult)
                p.tt(xnb[:, cs], xnb[:, cs], obuf[:, 0:tw], ALU.subtract)
            for c in range(DCH):
                p.dma(XN[c * 128:(c + 1) * 128, t0:t0 + tw], xnb[:, c * tw:(c + 1) * tw])

        WCOLS = {'fc': H, 'pr': D}
        def tern_stats(W, nrows, label):
            nch = nrows // 128
            per = 2560
            ncols = WCOLS[label]
            nslices = (ncols + per - 1) // per
            cnt = 0
            for c in range(nch):
                for sl in range(nslices):
                    c0 = sl * per
                    cw = min(per, ncols - c0)
                    p.dma(lbuf[:, 0:cw], W[c * 128:(c + 1) * 128, c0:c0 + cw])
                    p.act(s2[:, 0:cw], lbuf[:, 0:cw], AF.Abs)
                    p.rsum(cola[:, 4:5], s2[:, 0:cw])
                    p.mm(R0[0:1, 0:1], ones_col, cola[:, 4:5],
                         start=(cnt == 0), stop=(cnt == nch * nslices - 1))
                    cnt += 1
            tot = float(nrows) * float(ncols)
            p.ts(rows[0:1, 0:1], R0[0:1, 0:1], 1.0 / tot, ALU.mult, 1e-5, ALU.max)
            p.ts(rows[0:1, 1:2], rows[0:1, 0:1], 0.7, ALU.mult)
            p.mm(P0[:, 0:2], ones_row, rows[0:1, 0:2], start=True, stop=True)
            if label == 'fc':
                amc, thc = cola[:, 0:1], cola[:, 1:2]
            else:
                amc, thc = cola[:, 2:3], cola[:, 3:4]
            p.scopy(amc, P0[:, 0:1])
            p.scopy(thc, P0[:, 1:2])
            return amc, thc

        am_fc, th_fc = tern_stats(WFC, D, 'fc')
        am_pr, th_pr = tern_stats(WPR, H, 'pr')

        def tern_load(W, rch, c0, cw, dst_ap, amc, thc):
            p.dma(lbuf[:, 0:cw], W[rch * 128:(rch + 1) * 128, c0:c0 + cw])
            p.act(s2[:, 0:cw], lbuf[:, 0:cw], AF.Abs)
            p.act(s3[:, 0:cw], lbuf[:, 0:cw], AF.Sign)
            p.ts(s2[:, 0:cw], s2[:, 0:cw], thc, ALU.is_gt, amc, ALU.mult)
            p.tt(dst_ap, s3[:, 0:cw], s2[:, 0:cw], ALU.mult)

        QH = 1280
        for qi in range(4):
            h0 = qi * QH
            for c in range(DCH):
                tern_load(WFC, c, h0, QH, wbuf[:, c * QH:(c + 1) * QH], am_fc, th_fc)
            for ti in range(8):
                t0 = ti * 512
                tw = 512
                for c in range(DCH):
                    p.dma(xnb[:, c * tw:(c + 1) * tw], XN[c * 128:(c + 1) * 128, t0:t0 + tw])
                for j in range(QH // 128):
                    for c in range(DCH):
                        p.mm(P0[:, 0:tw], wbuf[:, c * QH + j * 128: c * QH + (j + 1) * 128],
                             xnb[:, c * tw:(c + 1) * tw], start=(c == 0), stop=(c == DCH - 1))
                    p.act(obuf[:, 0:tw], P0[:, 0:tw], AF.Relu)
                    p.tt(hhb[:, 0:tw], obuf[:, 0:tw], obuf[:, 0:tw], ALU.mult)
                    p.dma(HH[h0 + j * 128: h0 + (j + 1) * 128, t0:t0 + tw], hhb[:, 0:tw])

        QF = 512
        HCH = H // 128
        for qi in range(4):
            f0 = qi * QF
            for hc in range(HCH):
                tern_load(WPR, hc, f0, QF, wbuf[:, hc * QF:(hc + 1) * QF], am_pr, th_pr)
            for ti in range(16):
                t0 = ti * 256
                tw = 256
                for hc in range(HCH):
                    p.dma(hhb[:, hc * tw:(hc + 1) * tw], HH[hc * 128:(hc + 1) * 128, t0:t0 + tw])
                for j in range(QF // 128):
                    fr = f0 + j * 128
                    p.dma(xt[:, 0:tw], YIN[fr:fr + 128, t0:t0 + tw])
                    for hc in range(HCH):
                        p.mm(P0[:, 0:tw], wbuf[:, hc * QF + j * 128: hc * QF + (j + 1) * 128],
                             hhb[:, hc * tw:(hc + 1) * tw], start=(hc == 0), stop=(hc == HCH - 1))
                    p.tt(obuf[:, 0:tw], P0[:, 0:tw], xt[:, 0:tw], ALU.add)
                    p.dma(YOUT[fr:fr + 128, t0:t0 + tw], obuf[:, 0:tw])

        p.emit(nc, blk, S)
    return nc



def _shard_cols(a):
    return [np.ascontiguousarray(a[:, c * TPC:(c + 1) * TPC]) for c in range(NCORES)]


def run_l0ffn(y1T, w_fc, w_proj):
    nc = build_l0ffn()
    ones = np.ones((128, 128), np.float32)
    wfcT = np.ascontiguousarray(w_fc.T)
    wprT = np.ascontiguousarray(w_proj.T)
    in_maps = [{"yin": s, "wfc": wfcT, "wpr": wprT, "const": ones}
               for s in _shard_cols(y1T)]
    res = run_bass_kernel_spmd(nc, in_maps, core_ids=list(range(NCORES)))
    return np.concatenate([r["yout"] for r in res.results], axis=1)


def kernel(L0_attn_x, L0_attn_slope_rate, L0_attn_W_qkvu, L0_attn_W_out,
           L0_ffn_w_fc, L0_ffn_w_proj, L1_attn_W_qkv, L1_attn_W_o,
           L1_ffn_w_gate, L1_ffn_w_shared_up, L1_ffn_w_shared_down,
           L1_ffn_w_up, L1_ffn_w_down):
    x = np.asarray(L0_attn_x, dtype=np.float32)
    xT_shards = [np.ascontiguousarray(x[b].T) for b in range(NCORES)]
    y1T = run_l0attn(xT_shards, L0_attn_slope_rate, L0_attn_W_qkvu, L0_attn_W_out)
    y2T = run_l0ffn(y1T, L0_ffn_w_fc, L0_ffn_w_proj)
    y2_shards = [np.ascontiguousarray(y2T[:, c * TPC:(c + 1) * TPC])
                 for c in range(NCORES)]
    y3T = run_l1attn(y2_shards, L1_attn_W_qkv, L1_attn_W_o)
    y3T = np.ascontiguousarray(y3T)
    outr, dst6, w6 = run_moe(y3T, L1_ffn_w_gate, L1_ffn_w_up, L1_ffn_w_down)
    y3_shards = [np.ascontiguousarray(y3T[:, c * TPC:(c + 1) * TPC])
                 for c in range(NCORES)]
    y4T = run_comb(y3_shards, outr, dst6, w6,
                   L1_ffn_w_shared_up, L1_ffn_w_shared_down)
    out = np.ascontiguousarray(y4T.T).reshape(8, 4096, 2048).astype(np.float32)
    return out



def build_l0attn():
    nc = bass.Bass("TRN2", target_bir_lowering=False)
    XIN = nc.dram_tensor("xin", [D, TPC], F32, kind="ExternalInput").ap()
    SLOPE = nc.dram_tensor("slope", [1, 16], F32, kind="ExternalInput").ap()
    WQT = nc.dram_tensor("wqt", [D, 8192], F32, kind="ExternalInput").ap()
    WOT = nc.dram_tensor("wot", [D, D], F32, kind="ExternalInput").ap()
    CONST = nc.dram_tensor("const", [128, 128], F32, kind="ExternalInput").ap()
    ARRC = nc.dram_tensor("arrc", [128, 256], F32, kind="ExternalInput").ap()
    KIDC = nc.dram_tensor("kidc", [128, 2], F32, kind="ExternalInput").ap()
    MTC = nc.dram_tensor("mtc", [128, 512], F32, kind="ExternalInput").ap()
    EYEB = nc.dram_tensor("eyeb", [128, 128], BF16, kind="ExternalInput").ap()
    YOUT = nc.dram_tensor("yout", [D, TPC], F32, kind="ExternalOutput").ap()

    XNB_D = nc.dram_tensor("xnb_d", [D, TPC], BF16, kind="Internal").ap()
    QT_D = nc.dram_tensor("qt_d", [D, TPC], BF16, kind="Internal").ap()
    KT_D = nc.dram_tensor("kt_d", [D, TPC], BF16, kind="Internal").ap()
    UT_D = nc.dram_tensor("ut_d", [D, TPC], BF16, kind="Internal").ap()
    VT_D = nc.dram_tensor("vt_d", [TPC, D], BF16, kind="Internal").ap()
    O2_D = nc.dram_tensor("o2_d", [D, TPC], BF16, kind="Internal").ap()

    p = Prog()
    with ExitStack() as _st:
        ec = _st.enter_context
        ones = ec(nc.sbuf_tensor("ones", [128, 128], F32))
        eyeb = ec(nc.sbuf_tensor("eyeb_sb", [128, 128], BF16))
        arr = ec(nc.sbuf_tensor("arr", [128, 256], F32))
        kid = ec(nc.sbuf_tensor("kid", [128, 2], F32))
        mt = ec(nc.sbuf_tensor("mt", [128, 512], F32))
        slp = ec(nc.sbuf_tensor("slp", [1, 16], F32))
        scol = ec(nc.sbuf_tensor("scol", [128, 16], F32))
        bdc = ec(nc.sbuf_tensor("bdc", [128, 16], F32))
        kdc = ec(nc.sbuf_tensor("kdc", [128, 32], F32))
        qdb = ec(nc.sbuf_tensor("qdb", [128, 4096], BF16))
        dtb = ec(nc.sbuf_tensor("dtb", [128, 8192], BF16))
        dtmp = ec(nc.sbuf_tensor("dtmp", [128, 512], F32))
        AR1 = ec(nc.sbuf_tensor("AR1", [128, 8192], F32))
        AR2 = ec(nc.sbuf_tensor("AR2", [128, 16384], BF16))
        AR3 = ec(nc.sbuf_tensor("AR3", [128, 8192], BF16))
        lbuf = ec(nc.sbuf_tensor("lbuf", [128, 512], F32))
        sq = ec(nc.sbuf_tensor("sq", [128, 512], F32))
        ob2 = ec(nc.sbuf_tensor("ob2", [128, 512], F32))
        rows = ec(nc.sbuf_tensor("rows", [1, 2048], F32))
        rows2 = ec(nc.sbuf_tensor("rows2", [1, 2048], F32))
        KVT = ec(nc.sbuf_tensor("KVT", [128, 2048], F32))
        P0 = ec(nc.psum_tensor("P0", [128, 512], F32))
        P1 = ec(nc.psum_tensor("P1", [128, 512], F32))
        P2 = ec(nc.psum_tensor("P2", [128, 512], F32))
        R0 = ec(nc.psum_tensor("R0", [1, 512], F32))
        R1 = ec(nc.psum_tensor("R1", [1, 512], F32))
        PTB = ec(nc.psum_tensor("PTB", [128, 128], BF16))
        S = ec(nc.semaphore("S"))
        blk = ec(nc.Block())

        ones_col = ones[:, 0:1]
        ones_row = ones[0:1, :]

        p.dma(ones[:, :], CONST[:, :])
        p.dma(eyeb[:, :], EYEB[:, :])
        p.dma(arr[:, :], ARRC[:, :])
        p.dma(kid[:, :], KIDC[:, :])
        p.dma(mt[:, :], MTC[:, :])
        p.dma(slp[:, :], SLOPE[:, :])
        p.memset(KVT[:, :], 0.0)

        for h in range(16):
            p.mm(P2[:, 0:1], ones_row, slp[0:1, h:h + 1], start=True, stop=True)
            p.scopy(scol[:, h:h + 1], P2[:, 0:1])
            sc = scol[:, h:h + 1]
            p.ts(dtmp[:, 0:256], arr[:, :], sc, ALU.mult, -1.0, ALU.mult)
            p.act(qdb[:, h * 256:(h + 1) * 256], dtmp[:, 0:256], AF.Exp)
            p.ts(dtmp[:, 256:258], kid[:, :], sc, ALU.mult, -1.0, ALU.mult)
            p.act(kdc[:, 2 * h:2 * h + 2], dtmp[:, 256:258], AF.Exp)
            p.ts(dtmp[:, 258:259], sc, -256.0, ALU.mult)
            p.act(bdc[:, h:h + 1], dtmp[:, 258:259], AF.Exp)
            p.ts(dtmp[:, 0:512], mt[:, :], sc, ALU.mult, -1.0, ALU.mult)
            p.act(dtb[:, h * 512:(h + 1) * 512], dtmp[:, 0:512], AF.Exp)

        xt = AR1
        xnb = AR2
        for ti in range(8):
            t0 = ti * 512
            tw = 512
            for c in range(DCH):
                p.dma(xt[:, c * tw:(c + 1) * tw], XIN[c * 128:(c + 1) * 128, t0:t0 + tw])
            for c in range(DCH):
                cs = slice(c * tw, (c + 1) * tw)
                p.mm(R0[0:1, 0:tw], ones_col, xt[:, cs], start=(c == 0), stop=(c == DCH - 1))
                p.act(sq[:, 0:tw], xt[:, cs], AF.Square)
                p.mm(R1[0:1, 0:tw], ones_col, sq[:, 0:tw], start=(c == 0), stop=(c == DCH - 1))
            m = rows[0:1, 0:tw]
            v = rows2[0:1, 0:tw]
            p.ts(m, R0[0:1, 0:tw], 1.0 / 2048.0, ALU.mult)
            p.ts(v, R1[0:1, 0:tw], 1.0 / 2048.0, ALU.mult)
            mm_r = rows[0:1, 512:512 + tw]
            p.tt(mm_r, m, m, ALU.mult)
            p.tt(v, v, mm_r, ALU.subtract)
            p.ts(v, v, 1e-5, ALU.add)
            p.act(v, v, AF.Ln)
            p.act(v, v, AF.Exp, scale=-0.5)
            p.tt(mm_r, m, v, ALU.mult)
            p.mm(P0[:, 0:tw], ones_row, v, start=True, stop=True)
            p.mm(P2[:, 0:tw], ones_row, mm_r, start=True, stop=True)
            p.scopy(sq[:, 0:tw], P0[:, 0:tw])
            p.scopy(ob2[:, 0:tw], P2[:, 0:tw])
            for c in range(DCH):
                cs = slice(c * tw, (c + 1) * tw)
                p.stt(xnb[:, cs], xt[:, cs], 0.0, sq[:, 0:tw], ALU.add, ALU.mult)
                p.tt(xnb[:, cs], xnb[:, cs], ob2[:, 0:tw], ALU.subtract)
            for c in range(DCH):
                p.dma(XNB_D[c * 128:(c + 1) * 128, t0:t0 + tw], xnb[:, c * tw:(c + 1) * tw])

        wbf = AR3
        slices = []
        for i4 in range(4):
            slices.append((i4 * 512, QT_D, True))
        for i4 in range(4):
            slices.append((2048 + i4 * 512, KT_D, True))
        for i4 in range(4):
            slices.append((6144 + i4 * 512, UT_D, False))
        for (f0, DEST, dosilu) in slices:
            fl = f0 % 2048
            for c in range(DCH):
                p.dma(lbuf[:, :], WQT[c * 128:(c + 1) * 128, f0:f0 + 512])
                p.copy(wbf[:, c * 512:(c + 1) * 512], lbuf[:, :])
            for ti in range(8):
                t0 = ti * 512
                for c in range(DCH):
                    p.dma(xnb[:, c * 512:(c + 1) * 512],
                          XNB_D[c * 128:(c + 1) * 128, t0:t0 + 512])
                for j in range(4):
                    for c in range(DCH):
                        p.mm(P0[:, :], wbf[:, c * 512 + j * 128: c * 512 + (j + 1) * 128],
                             xnb[:, c * 512:(c + 1) * 512], start=(c == 0), stop=(c == DCH - 1))
                    dst_b = AR2[:, 8192:8704]
                    if dosilu:
                        p.act(dst_b, P0[:, :], AF.Silu)
                    else:
                        p.scopy(dst_b, P0[:, :])
                    p.dma(DEST[fl + j * 128: fl + (j + 1) * 128, t0:t0 + 512], dst_b)

        for es in range(4):
            f0 = 4096 + es * 512
            for c in range(DCH):
                p.dma(lbuf[:, :], WQT[c * 128:(c + 1) * 128, f0:f0 + 512])
                p.copy(wbf[:, c * 512:(c + 1) * 512], lbuf[:, :])
            for ti in range(8):
                t0 = ti * 512
                for c in range(DCH):
                    p.dma(xnb[:, c * 512:(c + 1) * 512],
                          XNB_D[c * 128:(c + 1) * 128, t0:t0 + 512])
                for tq in range(4):
                    for c in range(DCH):
                        p.mm(P0[:, :], xnb[:, c * 512 + tq * 128: c * 512 + (tq + 1) * 128],
                             wbf[:, c * 512:(c + 1) * 512], start=(c == 0), stop=(c == DCH - 1))
                    dst_b = AR2[:, 8192:8704]
                    p.scopy(dst_b, P0[:, :])
                    p.dma(VT_D[t0 + tq * 128: t0 + (tq + 1) * 128, es * 512:(es + 1) * 512], dst_b)

        qt = lambda h: AR2[:, h * 256:(h + 1) * 256]
        kt = lambda h: AR2[:, 4096 + h * 256:4096 + (h + 1) * 256]
        ubf = lambda c: AR2[:, 8192 + c * 256:8192 + (c + 1) * 256]
        obf = lambda c: AR2[:, 12288 + c * 256:12288 + (c + 1) * 256]
        vtok = lambda j2, h: AR3[:, j2 * 2048 + h * 128: j2 * 2048 + (h + 1) * 128]
        s_bf = AR3[:, 4096:4608]
        qp = AR3[:, 4608:4864]
        ktok = AR3[:, 4864:5120]
        kvt_bf = AR3[:, 5120:5248]
        kvbf = lambda h: AR3[:, 5248 + h * 128:5248 + (h + 1) * 128]
        OB = lambda c: AR1[:, c * 256:(c + 1) * 256]
        tmpf = AR1[:, 4096:4352]

        for blki in range(16):
            t0 = blki * 256
            for c in range(DCH):
                p.dma(AR2[:, c * 256:(c + 1) * 256], QT_D[c * 128:(c + 1) * 128, t0:t0 + 256])
                p.dma(AR2[:, 4096 + c * 256:4096 + (c + 1) * 256],
                      KT_D[c * 128:(c + 1) * 128, t0:t0 + 256])
                p.dma(AR2[:, 8192 + c * 256:8192 + (c + 1) * 256],
                      UT_D[c * 128:(c + 1) * 128, t0:t0 + 256])
            for j2 in range(2):
                p.dma(AR3[:, j2 * 2048:(j2 + 1) * 2048],
                      VT_D[t0 + j2 * 128:t0 + (j2 + 1) * 128, :])
            for h in range(16):
                p.mm(P0[:, 0:256], kt(h)[:, 0:128], qt(h), start=True, stop=True)
                p.mm(P1[:, 0:256], kt(h)[:, 128:256], qt(h), start=True, stop=True)
                p.copy(s_bf[:, 0:256], P0[:, 0:256])
                p.copy(s_bf[:, 256:512], P1[:, 0:256])
                p.tt(s_bf[:, 0:256], s_bf[:, 0:256], dtb[:, h * 512:h * 512 + 256], ALU.mult)
                p.tt(s_bf[:, 256:512], s_bf[:, 256:512], dtb[:, h * 512 + 256:h * 512 + 512], ALU.mult)
                p.tt(qp, qt(h), qdb[:, h * 256:(h + 1) * 256], ALU.mult)
                first = True
                if blki > 0:
                    p.mm(P2[:, 0:256], kvbf(h), qp, start=True, stop=False)
                    first = False
                p.mm(P2[:, 0:256], vtok(0, h), s_bf[:, 0:256], start=first, stop=False)
                p.mm(P2[:, 0:256], vtok(1, h), s_bf[:, 256:512], start=False, stop=True)
                p.scopy(OB(h), P2[:, 0:256])
                if blki < 15:
                    for j2 in range(2):
                        p.tr(PTB[:, :], kt(h)[:, j2 * 128:(j2 + 1) * 128], eyeb[:, :])
                        p.ts(ktok[:, j2 * 128:(j2 + 1) * 128], PTB[:, :],
                             kdc[:, 2 * h + j2:2 * h + j2 + 1], ALU.mult)
                    p.mm(P0[:, 0:128], vtok(0, h), ktok[:, 0:128], start=True, stop=False)
                    p.mm(P0[:, 0:128], vtok(1, h), ktok[:, 128:256], start=False, stop=True)
                    p.stt(KVT[:, h * 128:(h + 1) * 128], KVT[:, h * 128:(h + 1) * 128],
                          bdc[:, h:h + 1], P0[:, 0:128], ALU.mult, ALU.add)
                    p.copy(kvt_bf, KVT[:, h * 128:(h + 1) * 128])
                    p.tr(PTB[:, :], kvt_bf, eyeb[:, :])
                    p.copy(kvbf(h), PTB[:, :])
            for c in range(DCH):
                p.act(sq[:, 0:256], OB(c), AF.Square)
                p.mm(R0[0:1, 0:256], ones_col, sq[:, 0:256], start=(c == 0), stop=(c == DCH - 1))
            rr = rows[0:1, 0:256]
            p.ts(rr, R0[0:1, 0:256], 1.0 / 2048.0, ALU.mult, 1e-6, ALU.add)
            p.act(rr, rr, AF.Ln)
            p.act(rr, rr, AF.Exp, scale=-0.5)
            p.mm(P0[:, 0:256], ones_row, rr, start=True, stop=True)
            p.scopy(ob2[:, 0:256], P0[:, 0:256])
            for c in range(DCH):
                p.tt(s_bf[:, 0:256], OB(c), ob2[:, 0:256], ALU.mult)
                p.tt(obf(c), s_bf[:, 0:256], ubf(c), ALU.mult)
            for c in range(DCH):
                p.dma(O2_D[c * 128:(c + 1) * 128, t0:t0 + 256], obf(c))

        for fs in range(4):
            f0 = fs * 512
            for c in range(DCH):
                p.dma(lbuf[:, :], WOT[c * 128:(c + 1) * 128, f0:f0 + 512])
                p.copy(wbf[:, c * 512:(c + 1) * 512], lbuf[:, :])
            for ti in range(8):
                t0 = ti * 512
                for c in range(DCH):
                    p.dma(xnb[:, c * 512:(c + 1) * 512],
                          O2_D[c * 128:(c + 1) * 128, t0:t0 + 512])
                for j in range(4):
                    for c in range(DCH):
                        p.mm(P0[:, :], wbf[:, c * 512 + j * 128: c * 512 + (j + 1) * 128],
                             xnb[:, c * 512:(c + 1) * 512], start=(c == 0), stop=(c == DCH - 1))
                    p.dma(lbuf[:, :], XIN[f0 + j * 128: f0 + (j + 1) * 128, t0:t0 + 512])
                    p.tt(sq[:, 0:512], P0[:, :], lbuf[:, :], ALU.add)
                    p.dma(YOUT[f0 + j * 128: f0 + (j + 1) * 128, t0:t0 + 512], sq[:, 0:512])

        p.emit(nc, blk, S)
    return nc


def _l0attn_consts():
    arr = np.broadcast_to(np.arange(1, 257, dtype=np.float32)[None, :], (128, 256)).copy()
    kid = np.zeros((128, 2), np.float32)
    for c in range(2):
        kid[:, c] = 255.0 - (c * 128 + np.arange(128))
    mt = np.zeros((128, 512), np.float32)
    for j2 in range(2):
        for pp in range(128):
            t2 = j2 * 128 + pp
            for i in range(256):
                d = i - t2
                mt[pp, j2 * 256 + i] = d if d >= 0 else 1e9
    eyeb = np.eye(128).astype(BF16NP)
    return arr, kid, mt, eyeb


def run_l0attn(xT_shards, slope, W_qkvu, W_out):
    nc = build_l0attn()
    ones = np.ones((128, 128), np.float32)
    arr, kid, mt, eyeb = _l0attn_consts()
    wqt = np.ascontiguousarray(W_qkvu.T)
    wot = np.ascontiguousarray(W_out.T)
    slp = np.ascontiguousarray(slope.reshape(1, 16))
    in_maps = [{"xin": s, "slope": slp, "wqt": wqt, "wot": wot, "const": ones,
                "arrc": arr, "kidc": kid, "mtc": mt, "eyeb": eyeb}
               for s in xT_shards]
    res = run_bass_kernel_spmd(nc, in_maps, core_ids=list(range(NCORES)))
    return np.concatenate([r["yout"] for r in res.results], axis=1)



def build_l1attn():
    nc = bass.Bass("TRN2", target_bir_lowering=False)
    XIN = nc.dram_tensor("xin", [D, TPC], F32, kind="ExternalInput").ap()
    WQKT = nc.dram_tensor("wqkt", [D, 6144], F32, kind="ExternalInput").ap()
    WOT = nc.dram_tensor("wot", [D, D], F32, kind="ExternalInput").ap()
    CONST = nc.dram_tensor("const", [128, 128], F32, kind="ExternalInput").ap()
    PERMT = nc.dram_tensor("permt", [128, 128], BF16, kind="ExternalInput").ap()
    COSB = nc.dram_tensor("cosb", [128, TPC], BF16, kind="ExternalInput").ap()
    SINB = nc.dram_tensor("sinb", [128, TPC], BF16, kind="ExternalInput").ap()
    M01 = nc.dram_tensor("m01", [128, 2048], BF16, kind="ExternalInput").ap()
    YOUT = nc.dram_tensor("yout", [D, TPC], F32, kind="ExternalOutput").ap()

    XNB_D = nc.dram_tensor("xnb_d", [D, TPC], BF16, kind="Internal").ap()
    QR_D = nc.dram_tensor("qr_d", [D, TPC], BF16, kind="Internal").ap()
    KR_D = nc.dram_tensor("kr_d", [D, TPC], BF16, kind="Internal").ap()
    VT_D = nc.dram_tensor("vt_d", [TPC, D], BF16, kind="Internal").ap()
    AO_D = nc.dram_tensor("ao_d", [D, TPC], BF16, kind="Internal").ap()

    p = Prog()
    with ExitStack() as _st:
        ec = _st.enter_context
        ones = ec(nc.sbuf_tensor("ones", [128, 128], F32))
        permt = ec(nc.sbuf_tensor("permt_sb", [128, 128], BF16))
        cosb = ec(nc.sbuf_tensor("cosb_sb", [128, 512], BF16))
        sinb = ec(nc.sbuf_tensor("sinb_sb", [128, 512], BF16))
        m01 = ec(nc.sbuf_tensor("m01_sb", [128, 2048], BF16))
        AR1 = ec(nc.sbuf_tensor("AR1", [128, 8192], F32))
        AR2 = ec(nc.sbuf_tensor("AR2", [128, 16384], BF16))
        AR3 = ec(nc.sbuf_tensor("AR3", [128, 8192], BF16))
        PB = ec(nc.sbuf_tensor("PB", [128, 16384], BF16))
        lbuf = ec(nc.sbuf_tensor("lbuf", [128, 512], F32))
        sq = ec(nc.sbuf_tensor("sq", [128, 512], F32))
        ob2 = ec(nc.sbuf_tensor("ob2", [128, 512], F32))
        dpart = ec(nc.sbuf_tensor("dpart", [128, 512], F32))
        rows = ec(nc.sbuf_tensor("rows", [1, 2048], F32))
        rows2 = ec(nc.sbuf_tensor("rows2", [1, 2048], F32))
        P0 = ec(nc.psum_tensor("P0", [128, 512], F32))
        P1 = ec(nc.psum_tensor("P1", [128, 512], F32))
        P2 = ec(nc.psum_tensor("P2", [128, 512], F32))
        R0 = ec(nc.psum_tensor("R0", [1, 512], F32))
        R1 = ec(nc.psum_tensor("R1", [1, 512], F32))
        S = ec(nc.semaphore("S"))
        blk = ec(nc.Block())

        ones_col = ones[:, 0:1]
        ones_row = ones[0:1, :]
        p.dma(ones[:, :], CONST[:, :])
        p.dma(permt[:, :], PERMT[:, :])
        p.dma(m01[:, :], M01[:, :])

        xt = AR1
        xnb = AR2

        for ti in range(8):
            t0 = ti * 512
            tw = 512
            for c in range(DCH):
                p.dma(xt[:, c * tw:(c + 1) * tw], XIN[c * 128:(c + 1) * 128, t0:t0 + tw])
            for c in range(DCH):
                cs = slice(c * tw, (c + 1) * tw)
                p.mm(R0[0:1, 0:tw], ones_col, xt[:, cs], start=(c == 0), stop=(c == DCH - 1))
                p.act(sq[:, 0:tw], xt[:, cs], AF.Square)
                p.mm(R1[0:1, 0:tw], ones_col, sq[:, 0:tw], start=(c == 0), stop=(c == DCH - 1))
            m = rows[0:1, 0:tw]
            v = rows2[0:1, 0:tw]
            p.ts(m, R0[0:1, 0:tw], 1.0 / 2048.0, ALU.mult)
            p.ts(v, R1[0:1, 0:tw], 1.0 / 2048.0, ALU.mult)
            mm_r = rows[0:1, 512:512 + tw]
            p.tt(mm_r, m, m, ALU.mult)
            p.tt(v, v, mm_r, ALU.subtract)
            p.ts(v, v, 1e-5, ALU.add)
            p.act(v, v, AF.Ln)
            p.act(v, v, AF.Exp, scale=-0.5)
            p.tt(mm_r, m, v, ALU.mult)
            p.mm(P0[:, 0:tw], ones_row, v, start=True, stop=True)
            p.mm(P2[:, 0:tw], ones_row, mm_r, start=True, stop=True)
            p.scopy(sq[:, 0:tw], P0[:, 0:tw])
            p.scopy(ob2[:, 0:tw], P2[:, 0:tw])
            for c in range(DCH):
                cs = slice(c * tw, (c + 1) * tw)
                p.stt(xnb[:, cs], xt[:, cs], 0.0, sq[:, 0:tw], ALU.add, ALU.mult)
                p.tt(xnb[:, cs], xnb[:, cs], ob2[:, 0:tw], ALU.subtract)
            for c in range(DCH):
                p.dma(XNB_D[c * 128:(c + 1) * 128, t0:t0 + tw], xnb[:, c * tw:(c + 1) * tw])

        wbf = AR3
        qsb = AR2[:, 8192:8704]
        rotb = AR2[:, 8704:9216]
        ab = AR2[:, 9216:9728]
        bb = AR2[:, 9728:10240]
        for si in range(8):
            f0 = si * 512
            DEST = QR_D if si < 4 else KR_D
            fl = f0 % 2048
            for c in range(DCH):
                p.dma(lbuf[:, :], WQKT[c * 128:(c + 1) * 128, f0:f0 + 512])
                p.copy(wbf[:, c * 512:(c + 1) * 512], lbuf[:, :])
            for ti in range(8):
                t0 = ti * 512
                p.dma(cosb[:, :], COSB[:, t0:t0 + 512])
                p.dma(sinb[:, :], SINB[:, t0:t0 + 512])
                for c in range(DCH):
                    p.dma(xnb[:, c * 512:(c + 1) * 512],
                          XNB_D[c * 128:(c + 1) * 128, t0:t0 + 512])
                for j in range(4):
                    for c in range(DCH):
                        p.mm(P0[:, :], wbf[:, c * 512 + j * 128: c * 512 + (j + 1) * 128],
                             xnb[:, c * 512:(c + 1) * 512], start=(c == 0), stop=(c == DCH - 1))
                    p.copy(qsb, P0[:, :])
                    p.mm(P1[:, :], permt[:, :], qsb, start=True, stop=True)
                    p.copy(rotb, P1[:, :])
                    p.tt(ab, qsb, cosb[:, :], ALU.mult)
                    p.tt(bb, rotb, sinb[:, :], ALU.mult)
                    p.tt(ab, ab, bb, ALU.add)
                    p.dma(DEST[fl + j * 128: fl + (j + 1) * 128, t0:t0 + 512], ab)

        for es in range(4):
            f0 = 4096 + es * 512
            for c in range(DCH):
                p.dma(lbuf[:, :], WQKT[c * 128:(c + 1) * 128, f0:f0 + 512])
                p.copy(wbf[:, c * 512:(c + 1) * 512], lbuf[:, :])
            for ti in range(8):
                t0 = ti * 512
                for c in range(DCH):
                    p.dma(xnb[:, c * 512:(c + 1) * 512],
                          XNB_D[c * 128:(c + 1) * 128, t0:t0 + 512])
                for tq in range(4):
                    for c in range(DCH):
                        p.mm(P0[:, :], xnb[:, c * 512 + tq * 128: c * 512 + (tq + 1) * 128],
                             wbf[:, c * 512:(c + 1) * 512], start=(c == 0), stop=(c == DCH - 1))
                    p.scopy(qsb, P0[:, :])
                    p.dma(VT_D[t0 + tq * 128: t0 + (tq + 1) * 128, es * 512:(es + 1) * 512], qsb)

        qt = AR2[:, 0:512]
        SC = 1.0 / np.sqrt(128.0)
        for h in range(16):
            for g in range(8):
                t0 = g * 512
                nch = 4 * (g + 1)
                p.dma(qt, QR_D[h * 128:(h + 1) * 128, t0:t0 + 512])
                kt = AR2[:, 512:512 + nch * 128]
                p.dma(kt, KR_D[h * 128:(h + 1) * 128, 0:nch * 128])
                for pc in range((nch + 7) // 8):
                    r0 = pc * 8 * 128
                    rn = min(8 * 128, nch * 128 - r0)
                    p.dma(AR2[:, 4608 + pc * 1024:4608 + pc * 1024 + rn].rearrange(
                              "p (c e) -> p c e", e=128),
                          VT_D[r0:r0 + rn, h * 128:(h + 1) * 128].rearrange(
                              "(c p) e -> p c e", p=128))
                for t2c in range(nch):
                    p.mm(P0[:, :], kt[:, t2c * 128:(t2c + 1) * 128], qt,
                         start=True, stop=True)
                    pslot = PB[:, t2c * 512:(t2c + 1) * 512]
                    p.act(pslot, P0[:, :], AF.Exp, scale=SC)
                    dj = t2c - 4 * g
                    if dj >= 0:
                        p.tt(pslot, pslot, m01[:, dj * 512:(dj + 1) * 512], ALU.mult)
                    p.mm(P1[:, :], AR2[:, 4608 + t2c * 128:4608 + (t2c + 1) * 128],
                         pslot, start=(t2c == 0), stop=(t2c == nch - 1))
                pbv = PB[:, 0:nch * 512].rearrange("p (c t) -> p t c", c=nch)
                p.rsum(dpart[:, :], pbv)
                p.mm(R0[0:1, :], ones_col, dpart[:, :], start=True, stop=True)
                dr = rows[0:1, 0:512]
                p.recip(dr, R0[0:1, :])
                p.mm(P2[:, :], ones_row, dr, start=True, stop=True)
                p.scopy(ob2[:, :], P2[:, :])
                p.tt(sq[:, :], P1[:, :], ob2[:, :], ALU.mult)
                p.copy(qsb, sq[:, :])
                p.dma(AO_D[h * 128:(h + 1) * 128, t0:t0 + 512], qsb)

        for fs in range(4):
            f0 = fs * 512
            for c in range(DCH):
                p.dma(lbuf[:, :], WOT[c * 128:(c + 1) * 128, f0:f0 + 512])
                p.copy(wbf[:, c * 512:(c + 1) * 512], lbuf[:, :])
            for ti in range(8):
                t0 = ti * 512
                for c in range(DCH):
                    p.dma(xnb[:, c * 512:(c + 1) * 512],
                          AO_D[c * 128:(c + 1) * 128, t0:t0 + 512])
                for j in range(4):
                    for c in range(DCH):
                        p.mm(P0[:, :], wbf[:, c * 512 + j * 128: c * 512 + (j + 1) * 128],
                             xnb[:, c * 512:(c + 1) * 512], start=(c == 0), stop=(c == DCH - 1))
                    p.dma(lbuf[:, :], XIN[f0 + j * 128: f0 + (j + 1) * 128, t0:t0 + 512])
                    p.tt(sq[:, :], P0[:, :], lbuf[:, :], ALU.add)
                    p.dma(YOUT[f0 + j * 128: f0 + (j + 1) * 128, t0:t0 + 512], sq[:, :])

        p.emit(nc, blk, S)
    return nc


def _l1attn_consts():
    dk = 128
    inv = 1.0 / 10000.0 ** (np.arange(0, dk, 2, dtype=np.float64) / dk)
    ang = np.arange(TPC, dtype=np.float64)[:, None] * inv[None, :]
    ang = np.repeat(ang, 2, axis=-1)
    cosb = np.cos(ang).T.astype(BF16NP).copy()
    sinb = np.sin(ang).T.astype(BF16NP).copy()
    permt = np.zeros((128, 128), np.float32)
    for i in range(64):
        permt[2 * i + 1, 2 * i] = -1.0
        permt[2 * i, 2 * i + 1] = 1.0
    permt = permt.astype(BF16NP)
    m01 = np.zeros((128, 2048), np.float32)
    for dj in range(4):
        for pp in range(128):
            for jj in range(512):
                if jj >= dj * 128 + pp:
                    m01[pp, dj * 512 + jj] = 1.0
    m01 = m01.astype(BF16NP)
    return cosb, sinb, permt, m01


def run_l1attn(y2T_shards, W_qkv, W_o):
    nc = build_l1attn()
    ones = np.ones((128, 128), np.float32)
    cosb, sinb, permt, m01 = _l1attn_consts()
    wqkt = np.ascontiguousarray(W_qkv.T)
    wot = np.ascontiguousarray(W_o.T)
    in_maps = [{"xin": s, "wqkt": wqkt, "wot": wot, "const": ones,
                "permt": permt, "cosb": cosb, "sinb": sinb, "m01": m01}
               for s in y2T_shards]
    res = run_bass_kernel_spmd(nc, in_maps, core_ids=list(range(NCORES)))
    return np.concatenate([r["yout"] for r in res.results], axis=1)



NTOK = 32768

def build_moe():
    nc = bass.Bass("TRN2", target_bir_lowering=False)
    Y3F = nc.dram_tensor("y3f", [D, NTOK], F32, kind="ExternalInput").ap()
    WG = nc.dram_tensor("wg", [D, 64], F32, kind="ExternalInput").ap()
    WUP = nc.dram_tensor("wup", [8, D, 512], F32, kind="ExternalInput").ap()
    WDN = nc.dram_tensor("wdn", [8, 256, D], F32, kind="ExternalInput").ap()
    CONST = nc.dram_tensor("const", [128, 128], F32, kind="ExternalInput").ap()
    LT = nc.dram_tensor("lt", [128, 128], F32, kind="ExternalInput").ap()
    EB = nc.dram_tensor("eb", [128, 64], F32, kind="ExternalInput").ap()
    WB = nc.dram_tensor("wb", [128, 64], F32, kind="ExternalInput").ap()
    EYEB = nc.dram_tensor("eyeb", [128, 128], BF16, kind="ExternalInput").ap()
    SHIFT = nc.dram_tensor("shift", [1, 1], F32, kind="ExternalInput").ap()
    OUTR = nc.dram_tensor("outr", [8192, D], BF16, kind="ExternalOutput").ap()
    DSTO = nc.dram_tensor("dsto", [NTOK, 8], F32, kind="ExternalOutput").ap()
    W6O = nc.dram_tensor("w6o", [NTOK, 8], F32, kind="ExternalOutput").ap()
    XNB_D = nc.dram_tensor("xnb_d", [D, NTOK], BF16, kind="Internal").ap()
    XDISP = nc.dram_tensor("xdisp", [8192, D], BF16, kind="Internal").ap()

    p = Prog()
    with ExitStack() as _st:
        ec = _st.enter_context
        ones = ec(nc.sbuf_tensor("ones", [128, 128], F32))
        eyeb = ec(nc.sbuf_tensor("eyeb_sb", [128, 128], BF16))
        lt = ec(nc.sbuf_tensor("lt_sb", [128, 128], F32))
        eb = ec(nc.sbuf_tensor("eb_sb", [128, 64], F32))
        wb = ec(nc.sbuf_tensor("wb_sb", [128, 64], F32))
        shc = ec(nc.sbuf_tensor("shc", [128, 1], F32))
        wgb = ec(nc.sbuf_tensor("wgb", [128, 1024], BF16))
        AR1 = ec(nc.sbuf_tensor("AR1", [128, 8192], F32))
        xnb = ec(nc.sbuf_tensor("xnb_t", [128, 8192], BF16))
        xtok = ec(nc.sbuf_tensor("xtok", [128, 2048], BF16))
        lbuf = ec(nc.sbuf_tensor("lbuf", [128, 512], F32))
        sq = ec(nc.sbuf_tensor("sq", [128, 512], F32))
        ob2 = ec(nc.sbuf_tensor("ob2", [128, 512], F32))
        rows = ec(nc.sbuf_tensor("rows", [1, 2048], F32))
        rows2 = ec(nc.sbuf_tensor("rows2", [1, 2048], F32))
        sc = ec(nc.sbuf_tensor("sc", [128, 64], F32))
        m8 = ec(nc.sbuf_tensor("m8", [128, 8], F32))
        repl = ec(nc.sbuf_tensor("repl", [128, 64], F32))
        ssel = ec(nc.sbuf_tensor("ssel", [128, 64], F32))
        sel = ec(nc.sbuf_tensor("sel", [128, 64], F32))
        offs = ec(nc.sbuf_tensor("offs", [1, 64], F32))
        pk = ec(nc.sbuf_tensor("pk", [128, 64], F32))
        pw = ec(nc.sbuf_tensor("pw", [128, 64], F32))
        pk8 = ec(nc.sbuf_tensor("pk8", [128, 8], F32))
        pw8 = ec(nc.sbuf_tensor("pw8", [128, 8], F32))
        qf = ec(nc.sbuf_tensor("qf", [128, 8], F32))
        pi = ec(nc.sbuf_tensor("pi", [128, 8], F32))
        efl = ec(nc.sbuf_tensor("efl", [128, 8], F32))
        s6 = ec(nc.sbuf_tensor("s6", [128, 8], F32))
        w6 = ec(nc.sbuf_tensor("w6", [128, 8], F32))
        vg = ec(nc.sbuf_tensor("vg", [128, 8], F32))
        dst = ec(nc.sbuf_tensor("dst", [128, 8], F32))
        dl = ec(nc.sbuf_tensor("dl", [128, 8], F32))
        du = ec(nc.sbuf_tensor("du", [128, 8], U32))
        wsc = ec(nc.sbuf_tensor("wsc", [128, 1], F32))
        wr = ec(nc.sbuf_tensor("wr", [128, 1], F32))
        wub = ec(nc.sbuf_tensor("wub", [128, 8192], BF16))
        wdb = ec(nc.sbuf_tensor("wdb", [128, 4096], BF16))
        xeT = ec(nc.sbuf_tensor("xeT", [128, 16384], BF16))
        usb = ec(nc.sbuf_tensor("usb", [128, 2048], F32))
        gs = ec(nc.sbuf_tensor("gs", [128, 1024], F32))
        hbuf = ec(nc.sbuf_tensor("hbuf", [128, 2048], BF16))
        orow = ec(nc.sbuf_tensor("orow", [128, 2048], BF16))
        P0 = ec(nc.psum_tensor("P0", [128, 512], F32))
        P1 = ec(nc.psum_tensor("P1", [128, 512], F32))
        P2 = ec(nc.psum_tensor("P2", [128, 512], F32))
        R0 = ec(nc.psum_tensor("R0", [1, 512], F32))
        R1 = ec(nc.psum_tensor("R1", [1, 512], F32))
        PTB = ec(nc.psum_tensor("PTB", [128, 128], BF16))
        S = ec(nc.semaphore("S"))
        blk = ec(nc.Block())

        ones_col = ones[:, 0:1]
        ones_row = ones[0:1, :]
        p.dma(ones[:, :], CONST[:, :])
        p.dma(eyeb[:, :], EYEB[:, :])
        p.dma(lt[:, :], LT[:, :])
        p.dma(eb[:, :], EB[:, :])
        p.dma(wb[:, :], WB[:, :])
        p.memset(offs[:, :], 0.0)
        p.dma(rows[0:1, 0:1], SHIFT[:, :])
        p.mm(P2[:, 0:1], ones_row, rows[0:1, 0:1], start=True, stop=True)
        p.scopy(shc[:, :], P2[:, 0:1])
        for c in range(DCH):
            p.dma(lbuf[:, 0:64], WG[c * 128:(c + 1) * 128, :])
            p.copy(wgb[:, c * 64:(c + 1) * 64], lbuf[:, 0:64])
        p.memset(xtok[:, :], 0.0)
        for s0 in range(64):
            p.dma(XDISP[s0 * 128:(s0 + 1) * 128, :], xtok[:, :])

        xt = AR1
        for ti in range(64):
            t0 = ti * 512
            tw = 512
            for c in range(DCH):
                p.dma(xt[:, c * tw:(c + 1) * tw], Y3F[c * 128:(c + 1) * 128, t0:t0 + tw])
            for c in range(DCH):
                cs = slice(c * tw, (c + 1) * tw)
                p.mm(R0[0:1, 0:tw], ones_col, xt[:, cs], start=(c == 0), stop=(c == DCH - 1))
                p.act(sq[:, 0:tw], xt[:, cs], AF.Square)
                p.mm(R1[0:1, 0:tw], ones_col, sq[:, 0:tw], start=(c == 0), stop=(c == DCH - 1))
            m = rows[0:1, 0:tw]
            v = rows2[0:1, 0:tw]
            p.ts(m, R0[0:1, 0:tw], 1.0 / 2048.0, ALU.mult)
            p.ts(v, R1[0:1, 0:tw], 1.0 / 2048.0, ALU.mult)
            mm_r = rows[0:1, 512:512 + tw]
            p.tt(mm_r, m, m, ALU.mult)
            p.tt(v, v, mm_r, ALU.subtract)
            p.ts(v, v, 1e-5, ALU.add)
            p.act(v, v, AF.Ln)
            p.act(v, v, AF.Exp, scale=-0.5)
            p.tt(mm_r, m, v, ALU.mult)
            p.mm(P0[:, 0:tw], ones_row, v, start=True, stop=True)
            p.mm(P2[:, 0:tw], ones_row, mm_r, start=True, stop=True)
            p.scopy(sq[:, 0:tw], P0[:, 0:tw])
            p.scopy(ob2[:, 0:tw], P2[:, 0:tw])
            for c in range(DCH):
                cs = slice(c * tw, (c + 1) * tw)
                p.stt(xnb[:, cs], xt[:, cs], 0.0, sq[:, 0:tw], ALU.add, ALU.mult)
                p.tt(xnb[:, cs], xnb[:, cs], ob2[:, 0:tw], ALU.subtract)
            for c in range(DCH):
                p.dma(XNB_D[c * 128:(c + 1) * 128, t0:t0 + tw], xnb[:, c * tw:(c + 1) * tw])

        for ch in range(256):
            t0 = ch * 128
            for c in range(DCH):
                p.dma(xnb[:, c * 128:(c + 1) * 128],
                      XNB_D[c * 128:(c + 1) * 128, t0:t0 + 128])
            for c in range(DCH):
                p.mm(P0[:, 0:64], xnb[:, c * 128:(c + 1) * 128],
                     wgb[:, c * 64:(c + 1) * 64], start=(c == 0), stop=(c == DCH - 1))
            p.act(sc[:, :], P0[:, 0:64], AF.Sigmoid)
            p.max8(m8[:, :], sc[:, :])
            p.memset(m8[:, 6:8], 0.0)
            p.mrep(repl[:, :], m8[:, :], sc[:, :], 0.0)
            p.tt(ssel[:, :], sc[:, :], repl[:, :], ALU.subtract)
            p.rsum(wsc[:, :], ssel[:, :])
            p.ts(sel[:, :], ssel[:, :], 0.0, ALU.is_gt)
            p.mm(P1[:, 0:64], lt[:, :], sel[:, :], start=True, stop=False)
            p.mm(P1[:, 0:64], ones_row, offs[:, :], start=False, stop=True)
            p.mm(R0[0:1, 0:64], ones_col, sel[:, :], start=True, stop=True)
            p.tt(pk[:, :], P1[:, 0:64], eb[:, :], ALU.add)
            p.tt(pk[:, :], pk[:, :], sel[:, :], ALU.mult)
            p.tt(pw[:, :], ssel[:, :], wb[:, :], ALU.add)
            p.tt(pw[:, :], pw[:, :], sel[:, :], ALU.mult)
            p.tt(offs[:, :], offs[:, :], R0[0:1, 0:64], ALU.add)
            p.max8(pk8[:, :], pk[:, :])
            p.max8(pw8[:, :], pw[:, :])
            p.ts(qf[:, 0:6], pk8[:, 0:6], 1.0 / 65536.0, ALU.mult, -0.5, ALU.add)
            p.copy(du[:, 0:6], qf[:, 0:6])
            p.copy(qf[:, 0:6], du[:, 0:6])
            p.stt(pi[:, 0:6], qf[:, 0:6], -65536.0, pk8[:, 0:6], ALU.mult, ALU.add)
            p.ts(efl[:, 0:6], qf[:, 0:6], -1.0, ALU.mult, 63.0, ALU.add)
            p.stt(s6[:, 0:6], qf[:, 0:6], -4.0, pw8[:, 0:6], ALU.mult, ALU.add)
            p.recip(wr[:, :], wsc[:, :])
            p.ts(w6[:, 0:6], s6[:, 0:6], wr[:, :], ALU.mult)
            p.ts(vg[:, 0:6], pi[:, 0:6], 1024.0, ALU.is_gt)
            p.stt(dst[:, 0:6], efl[:, 0:6], 1024.0, pi[:, 0:6], ALU.mult, ALU.add)
            p.ts(dst[:, 0:6], dst[:, 0:6], -1.0, ALU.add)
            p.stt(dst[:, 0:6], vg[:, 0:6], 1e7, dst[:, 0:6], ALU.mult, ALU.add)
            p.dma(DSTO[t0:t0 + 128, 0:6], dst[:, 0:6])
            p.dma(W6O[t0:t0 + 128, 0:6], w6[:, 0:6])
            p.ts(dl[:, 0:6], dst[:, 0:6], shc[:, :], ALU.subtract)
            p.ts(vg[:, 0:6], dl[:, 0:6], -0.5, ALU.is_lt)
            p.stt(dl[:, 0:6], vg[:, 0:6], 1e7, dl[:, 0:6], ALU.mult, ALU.add)
            p.copy(du[:, 0:6], dl[:, 0:6])
            for c in range(DCH):
                p.tr(PTB[:, :], xnb[:, c * 128:(c + 1) * 128], eyeb[:, :])
                p.copy(xtok[:, c * 128:(c + 1) * 128], PTB[:, :])
            for r in range(6):
                p.idma_scatter(XDISP[:, :], xtok[:, :], du[:, r:r + 1], 8191)

        for el in range(8):
            for s0 in range(8):
                p.dma(xtok[:, :], XDISP[el * 1024 + s0 * 128: el * 1024 + (s0 + 1) * 128, :])
                for c in range(DCH):
                    p.tr(PTB[:, :], xtok[:, c * 128:(c + 1) * 128], eyeb[:, :])
                    p.copy(xeT[:, c * 1024 + s0 * 128: c * 1024 + (s0 + 1) * 128], PTB[:, :])
            for c in range(DCH):
                p.dma(lbuf[:, :], WUP[el, c * 128:(c + 1) * 128, :])
                p.copy(wub[:, c * 512:(c + 1) * 512], lbuf[:, :])
            for chh in range(2):
                c0 = chh * 512
                for hj in range(4):
                    for c in range(DCH):
                        p.mm(P0[:, :], wub[:, c * 512 + hj * 128: c * 512 + (hj + 1) * 128],
                             xeT[:, c * 1024 + c0: c * 1024 + c0 + 512],
                             start=(c == 0), stop=(c == DCH - 1))
                    p.scopy(usb[:, hj * 512:(hj + 1) * 512], P0[:, :])
                p.act(gs[:, :], usb[:, 0:1024], AF.Silu)
                p.tt(hbuf[:, 0 * 1024 + c0: 0 * 1024 + c0 + 512], gs[:, 0:512],
                     usb[:, 1024:1536], ALU.mult)
                p.tt(hbuf[:, 1 * 1024 + c0: 1 * 1024 + c0 + 512], gs[:, 512:1024],
                     usb[:, 1536:2048], ALU.mult)
            for j in range(2):
                for fq in range(4):
                    p.dma(lbuf[:, :], WDN[el, j * 128:(j + 1) * 128, fq * 512:(fq + 1) * 512])
                    p.copy(wdb[:, j * 2048 + fq * 512: j * 2048 + (fq + 1) * 512], lbuf[:, :])
            for cc in range(8):
                for fj in range(4):
                    for j in range(2):
                        p.mm(P0[:, :], hbuf[:, j * 1024 + cc * 128:(j * 1024 + (cc + 1) * 128)],
                             wdb[:, j * 2048 + fj * 512: j * 2048 + (fj + 1) * 512],
                             start=(j == 0), stop=(j == 1))
                    p.copy(orow[:, fj * 512:(fj + 1) * 512], P0[:, :])
                p.dma(OUTR[el * 1024 + cc * 128: el * 1024 + (cc + 1) * 128, :], orow[:, :])

        p.emit(nc, blk, S)
    return nc


def _moe_consts():
    lt = np.zeros((128, 128), np.float32)
    for j in range(128):
        lt[j, j:] = 1.0
    e = np.arange(64, dtype=np.float32)
    eb = np.broadcast_to(((63.0 - e) * 65536.0)[None, :], (128, 64)).copy()
    wbc = np.broadcast_to(((63.0 - e) * 4.0)[None, :], (128, 64)).copy()
    return lt, eb, wbc


def run_moe(y3T, w_gate, w_up, w_down):
    nc = build_moe()
    ones = np.ones((128, 128), np.float32)
    eyeb = np.eye(128).astype(BF16NP)
    lt, eb, wbc = _moe_consts()
    in_maps = []
    for c in range(NCORES):
        in_maps.append({
            "y3f": y3T, "wg": np.ascontiguousarray(w_gate),
            "wup": np.ascontiguousarray(w_up[c * 8:(c + 1) * 8]),
            "wdn": np.ascontiguousarray(w_down[c * 8:(c + 1) * 8]),
            "const": ones, "lt": lt, "eb": eb, "wb": wbc, "eyeb": eyeb,
            "shift": np.full((1, 1), 8192.0 * c, np.float32),
        })
    res = run_bass_kernel_spmd(nc, in_maps, core_ids=list(range(NCORES)))
    outr = np.concatenate([r["outr"] for r in res.results], axis=0)
    dst6 = res.results[0]["dsto"]
    w6 = res.results[0]["w6o"]
    return outr, dst6, w6



def build_comb():
    nc = bass.Bass("TRN2", target_bir_lowering=False)
    Y3 = nc.dram_tensor("y3", [D, TPC], F32, kind="ExternalInput").ap()
    OUTR = nc.dram_tensor("outr", [65536, D], BF16, kind="ExternalInput").ap()
    DSTI = nc.dram_tensor("dsti", [TPC, 8], F32, kind="ExternalInput").ap()
    W6I = nc.dram_tensor("w6i", [TPC, 8], F32, kind="ExternalInput").ap()
    WSU = nc.dram_tensor("wsu", [D, 1024], F32, kind="ExternalInput").ap()
    WSD = nc.dram_tensor("wsd", [512, D], F32, kind="ExternalInput").ap()
    CONST = nc.dram_tensor("const", [128, 128], F32, kind="ExternalInput").ap()
    EYEF = nc.dram_tensor("eyef", [128, 128], F32, kind="ExternalInput").ap()
    YOUT = nc.dram_tensor("yout", [D, TPC], F32, kind="ExternalOutput").ap()

    p = Prog()
    with ExitStack() as _st:
        ec = _st.enter_context
        ones = ec(nc.sbuf_tensor("ones", [128, 128], F32))
        eye = ec(nc.sbuf_tensor("eye_sb", [128, 128], F32))
        wsu = ec(nc.sbuf_tensor("wsu_sb", [128, 16384], BF16))
        wsd = ec(nc.sbuf_tensor("wsd_sb", [128, 8192], BF16))
        AR1 = ec(nc.sbuf_tensor("AR1", [128, 8192], F32))
        xnb = ec(nc.sbuf_tensor("xnb_t", [128, 8192], BF16))
        ssb = ec(nc.sbuf_tensor("ssb", [128, 4096], F32))
        gsl = ec(nc.sbuf_tensor("gsl", [128, 2048], F32))
        shh = ec(nc.sbuf_tensor("shh", [128, 2048], BF16))
        shsb = ec(nc.sbuf_tensor("shsb", [128, 8192], F32))
        racc = ec(nc.sbuf_tensor("racc", [128, 2048], F32))
        grow = ec(nc.sbuf_tensor("grow", [128, 2048], F32))
        dstf = ec(nc.sbuf_tensor("dstf", [128, 8], F32))
        du = ec(nc.sbuf_tensor("du", [128, 8], U32))
        w6s = ec(nc.sbuf_tensor("w6s", [128, 8], F32))
        lbuf = ec(nc.sbuf_tensor("lbuf", [128, 1024], F32))
        sq = ec(nc.sbuf_tensor("sq", [128, 512], F32))
        ob2 = ec(nc.sbuf_tensor("ob2", [128, 512], F32))
        tmpf = ec(nc.sbuf_tensor("tmpf", [128, 128], F32))
        rows = ec(nc.sbuf_tensor("rows", [1, 2048], F32))
        rows2 = ec(nc.sbuf_tensor("rows2", [1, 2048], F32))
        P0 = ec(nc.psum_tensor("P0", [128, 512], F32))
        P1 = ec(nc.psum_tensor("P1", [128, 512], F32))
        P2 = ec(nc.psum_tensor("P2", [128, 512], F32))
        R0 = ec(nc.psum_tensor("R0", [1, 512], F32))
        R1 = ec(nc.psum_tensor("R1", [1, 512], F32))
        PTF = ec(nc.psum_tensor("PTF", [128, 128], F32))
        S = ec(nc.semaphore("S"))
        blk = ec(nc.Block())

        ones_col = ones[:, 0:1]
        ones_row = ones[0:1, :]
        p.dma(ones[:, :], CONST[:, :])
        p.dma(eye[:, :], EYEF[:, :])
        for c in range(DCH):
            p.dma(lbuf[:, :], WSU[c * 128:(c + 1) * 128, :])
            p.copy(wsu[:, c * 1024:(c + 1) * 1024], lbuf[:, :])
        for j in range(4):
            p.dma(lbuf[:, :], WSD[j * 128:(j + 1) * 128, 0:1024])
            p.copy(wsd[:, j * 2048:j * 2048 + 1024], lbuf[:, :])
            p.dma(lbuf[:, :], WSD[j * 128:(j + 1) * 128, 1024:2048])
            p.copy(wsd[:, j * 2048 + 1024:(j + 1) * 2048], lbuf[:, :])

        xt = AR1
        for ti in range(8):
            t0 = ti * 512
            tw = 512
            for c in range(DCH):
                p.dma(xt[:, c * tw:(c + 1) * tw], Y3[c * 128:(c + 1) * 128, t0:t0 + tw])
            for c in range(DCH):
                cs = slice(c * tw, (c + 1) * tw)
                p.mm(R0[0:1, 0:tw], ones_col, xt[:, cs], start=(c == 0), stop=(c == DCH - 1))
                p.act(sq[:, 0:tw], xt[:, cs], AF.Square)
                p.mm(R1[0:1, 0:tw], ones_col, sq[:, 0:tw], start=(c == 0), stop=(c == DCH - 1))
            m = rows[0:1, 0:tw]
            v = rows2[0:1, 0:tw]
            p.ts(m, R0[0:1, 0:tw], 1.0 / 2048.0, ALU.mult)
            p.ts(v, R1[0:1, 0:tw], 1.0 / 2048.0, ALU.mult)
            mm_r = rows[0:1, 512:512 + tw]
            p.tt(mm_r, m, m, ALU.mult)
            p.tt(v, v, mm_r, ALU.subtract)
            p.ts(v, v, 1e-5, ALU.add)
            p.act(v, v, AF.Ln)
            p.act(v, v, AF.Exp, scale=-0.5)
            p.tt(mm_r, m, v, ALU.mult)
            p.mm(P0[:, 0:tw], ones_row, v, start=True, stop=True)
            p.mm(P2[:, 0:tw], ones_row, mm_r, start=True, stop=True)
            p.scopy(sq[:, 0:tw], P0[:, 0:tw])
            p.scopy(ob2[:, 0:tw], P2[:, 0:tw])
            for c in range(DCH):
                cs = slice(c * tw, (c + 1) * tw)
                p.stt(xnb[:, cs], xt[:, cs], 0.0, sq[:, 0:tw], ALU.add, ALU.mult)
                p.tt(xnb[:, cs], xnb[:, cs], ob2[:, 0:tw], ALU.subtract)
            for hj in range(8):
                for c in range(DCH):
                    p.mm(P0[:, :], wsu[:, c * 1024 + hj * 128: c * 1024 + (hj + 1) * 128],
                         xnb[:, c * 512:(c + 1) * 512], start=(c == 0), stop=(c == DCH - 1))
                p.scopy(ssb[:, hj * 512:(hj + 1) * 512], P0[:, :])
            p.act(gsl[:, :], ssb[:, 2048:4096], AF.Silu)
            p.tt(shh[:, :], gsl[:, :], ssb[:, 0:2048], ALU.mult)
            for fj in range(16):
                for j in range(4):
                    p.mm(P1[:, :], wsd[:, j * 2048 + fj * 128: j * 2048 + (fj + 1) * 128],
                         shh[:, j * 512:(j + 1) * 512], start=(j == 0), stop=(j == 3))
                p.scopy(shsb[:, fj * 512:(fj + 1) * 512], P1[:, :])
            for tq in range(4):
                tq0 = t0 + tq * 128
                p.dma(dstf[:, 0:6], DSTI[tq0:tq0 + 128, 0:6])
                p.dma(w6s[:, 0:6], W6I[tq0:tq0 + 128, 0:6])
                p.copy(du[:, 0:6], dstf[:, 0:6])
                for r in range(6):
                    p.memset(grow[:, :], 0.0)
                    p.idma_gather(grow[:, :], OUTR[:, :], du[:, r:r + 1], 65535)
                    if r == 0:
                        p.ts(racc[:, :], grow[:, :], w6s[:, 0:1], ALU.mult)
                    else:
                        p.stt(racc[:, :], grow[:, :], w6s[:, r:r + 1], racc[:, :],
                              ALU.mult, ALU.add)
                for c in range(DCH):
                    p.tr(PTF[:, :], racc[:, c * 128:(c + 1) * 128], eye[:, :])
                    p.tt(tmpf[:, :], PTF[:, :], shsb[:, c * 512 + tq * 128: c * 512 + (tq + 1) * 128], ALU.add)
                    p.tt(tmpf[:, :], tmpf[:, :], xt[:, c * 512 + tq * 128: c * 512 + (tq + 1) * 128], ALU.add)
                    p.dma(YOUT[c * 128:(c + 1) * 128, tq0:tq0 + 128], tmpf[:, :])

        p.emit(nc, blk, S)
    return nc


def run_comb(y3T_shards, outr, dst6, w6, w_su, w_sd):
    nc = build_comb()
    ones = np.ones((128, 128), np.float32)
    eyef = np.eye(128, dtype=np.float32)
    in_maps = []
    for c in range(NCORES):
        in_maps.append({
            "y3": y3T_shards[c], "outr": outr,
            "dsti": np.ascontiguousarray(dst6[c * TPC:(c + 1) * TPC]),
            "w6i": np.ascontiguousarray(w6[c * TPC:(c + 1) * TPC]),
            "wsu": np.ascontiguousarray(w_su), "wsd": np.ascontiguousarray(w_sd),
            "const": ones, "eyef": eyef,
        })
    res = run_bass_kernel_spmd(nc, in_maps, core_ids=list(range(NCORES)))
    return np.concatenate([r["yout"] for r in res.results], axis=1)


# version 02 (final)
# speedup vs baseline: 1.1596x; 1.1596x over previous
import numpy as np
import ml_dtypes
from contextlib import ExitStack
import concourse.bass as bass
import concourse.mybir as mybir
from concourse.bass_utils import run_bass_kernel_spmd
from concourse.bass_core_rust import IndirectOffsetOnAxis

dt = mybir.dt
AF = mybir.ActivationFunctionType
ALU = mybir.AluOpType
AX = mybir.AxisListType.X
F32 = dt.float32
BF16 = dt.bfloat16
U32 = dt.uint32

NCORES = 8
D = 2048
TPC = 4096
DCH = 16

BF16NP = ml_dtypes.bfloat16


class Prog:

    def __init__(self):
        self.ops = []

    def _add(self, eng, fn, inc=1):
        self.ops.append((eng, fn, inc))

    def dma(self, out, in_, q='sync'):
        self._add(q, lambda e: e.dma_start(out=out, in_=in_), 16)

    def idma_gather(self, out, in_, idx_ap, bc):
        def f(e):
            r = self._bc_reg(e, bc)
            return e.indirect_dma_start(
                out=out, out_offset=None, in_=in_,
                in_offset=IndirectOffsetOnAxis(ap=idx_ap, axis=0),
                bounds_check=r, oob_is_err=False)
        self._add('gpsimd', f, 16)

    def idma_scatter(self, out, in_, idx_ap, bc):
        def f(e):
            r = self._bc_reg(e, bc)
            return e.indirect_dma_start(
                out=out, out_offset=IndirectOffsetOnAxis(ap=idx_ap, axis=0),
                in_=in_, in_offset=None,
                bounds_check=r, oob_is_err=False)
        self._add('gpsimd', f, 16)

    def _bc_reg(self, e, bc):
        if not hasattr(self, '_bcregs'):
            self._bcregs = {}
        if bc not in self._bcregs:
            self._bcregs[bc] = e.to_reg(bc)
        return self._bcregs[bc]

    def mm(self, out, lhsT, rhs, start, stop):
        self._add('tensor', lambda e: e.matmul(out=out, lhsT=lhsT, rhs=rhs,
                                               start=start, stop=stop))

    def tr(self, out, in_, ident):
        self._add('tensor', lambda e: e.transpose(out=out, in_=in_, identity=ident))

    def act(self, out, in_, func, bias=None, scale=None):
        kw = {}
        if bias is not None:
            kw['bias'] = bias
        if scale is not None:
            kw['scale'] = scale
        self._add('scalar', lambda e: e.activation(out=out, in_=in_, func=func, **kw))

    def scopy(self, out, in_):
        self._add('scalar', lambda e: e.copy(out=out, in_=in_))

    def copy(self, out, in_):
        self._add('vector', lambda e: e.tensor_copy(out=out, in_=in_))

    def tt(self, out, in0, in1, op):
        self._add('vector', lambda e: e.tensor_tensor(out=out, in0=in0, in1=in1, op=op))

    def ts(self, out, in0, s1, op0, s2=0.0, op1=ALU.add):
        self._add('vector', lambda e: e.tensor_scalar(out=out, in0=in0, scalar1=s1,
                                                      scalar2=s2, op0=op0, op1=op1))

    def stt(self, out, in0, s, in1, op0, op1):
        self._add('vector', lambda e: e.scalar_tensor_tensor(
            out=out, in0=in0, scalar=s, in1=in1, op0=op0, op1=op1))

    def rsum(self, out, in_):
        self._add('vector', lambda e: e.reduce_sum(out=out, in_=in_, axis=AX))

    def rmax(self, out, in_):
        self._add('vector', lambda e: e.reduce_max(out=out, in_=in_, axis=AX))

    def max8(self, out, in_):
        self._add('vector', lambda e: e.max(out=out, in_=in_))

    def mrep(self, out, vals8, in_values, imm=0.0):
        self._add('vector', lambda e: e.match_replace(out=out, in_to_replace=vals8,
                                                      in_values=in_values, imm_value=imm))

    def recip(self, out, in_):
        self._add('vector', lambda e: e.reciprocal(out=out, in_=in_))

    def memset(self, ap, v):
        self._add('vector', lambda e: e.memset(ap, v))

    def emit(self, nc, blk, S):
        totals = []
        waits = []
        t = 0
        prev_en = None
        prev_dma = True
        for (en, _, inc) in self.ops:
            is_dma = (inc == 16)
            if en == 'tensor' and prev_en == 'tensor' and not is_dma and not prev_dma:
                waits.append(None)
            else:
                waits.append(t)
            totals.append(t)
            t += inc
            prev_en = en
            prev_dma = is_dma
        final = t
        engs = ['sync', 'tensor', 'vector', 'scalar', 'gpsimd']

        def mk(section):
            def body(eng):
                for (en, fn, inc), w in zip(self.ops, waits):
                    if en != section:
                        continue
                    if w is not None:
                        eng.wait_ge(S, w)
                    fn(eng).then_inc(S, inc)
                if section == 'sync':
                    eng.wait_ge(S, final)
            return body

        for en in engs:
            getattr(blk, en)(mk(en))



def build_l0ffn():
    nc = bass.Bass("TRN2", target_bir_lowering=False)
    H = 5120
    YIN = nc.dram_tensor("yin", [D, TPC], F32, kind="ExternalInput").ap()
    WFC = nc.dram_tensor("wfc", [D, H], F32, kind="ExternalInput").ap()
    WPR = nc.dram_tensor("wpr", [H, D], F32, kind="ExternalInput").ap()
    CONST = nc.dram_tensor("const", [128, 128], F32, kind="ExternalInput").ap()
    YOUT = nc.dram_tensor("yout", [D, TPC], F32, kind="ExternalOutput").ap()
    XN = nc.dram_tensor("xn_scr", [D, TPC], BF16, kind="Internal").ap()
    HH = nc.dram_tensor("hh_scr", [H, TPC], BF16, kind="Internal").ap()

    p = Prog()
    with ExitStack() as _st:
        ec = _st.enter_context
        ones = ec(nc.sbuf_tensor("ones", [128, 128], F32))
        wbuf = ec(nc.sbuf_tensor("wbuf", [128, 32768], BF16))
        lbuf = ec(nc.sbuf_tensor("lbuf", [128, 2560], F32))
        s2 = ec(nc.sbuf_tensor("s2", [128, 2560], F32))
        s3 = ec(nc.sbuf_tensor("s3", [128, 2560], F32))
        xt = ec(nc.sbuf_tensor("xt", [128, 8192], F32))
        xnb = ec(nc.sbuf_tensor("xnb", [128, 8192], BF16))
        hhb = ec(nc.sbuf_tensor("hhb", [128, 10240], BF16))
        sq = ec(nc.sbuf_tensor("sq", [128, 512], F32))
        obuf = ec(nc.sbuf_tensor("obuf", [128, 512], F32))
        rows = ec(nc.sbuf_tensor("rows", [1, 4096], F32))
        rows2 = ec(nc.sbuf_tensor("rows2", [1, 4096], F32))
        cola = ec(nc.sbuf_tensor("cola", [128, 8], F32))
        P0 = ec(nc.psum_tensor("P0", [128, 512], F32))
        P2 = ec(nc.psum_tensor("P2", [128, 512], F32))
        R0 = ec(nc.psum_tensor("R0", [1, 512], F32))
        R1 = ec(nc.psum_tensor("R1", [1, 512], F32))
        S = ec(nc.semaphore("S"))
        blk = ec(nc.Block())
        ones_col = ones[:, 0:1]
        ones_row = ones[0:1, :]

        p.dma(ones[:, :], CONST[:, :])

        for ti in range(8):
            t0 = ti * 512
            tw = 512
            for c in range(DCH):
                p.dma(xt[:, c * tw:(c + 1) * tw], YIN[c * 128:(c + 1) * 128, t0:t0 + tw])
            for c in range(DCH):
                cs = slice(c * tw, (c + 1) * tw)
                p.mm(R0[0:1, 0:tw], ones_col, xt[:, cs], start=(c == 0), stop=(c == DCH - 1))
                p.act(sq[:, 0:tw], xt[:, cs], AF.Square)
                p.mm(R1[0:1, 0:tw], ones_col, sq[:, 0:tw], start=(c == 0), stop=(c == DCH - 1))
            m = rows[0:1, 0:tw]
            v = rows2[0:1, 0:tw]
            p.ts(m, R0[0:1, 0:tw], 1.0 / 2048.0, ALU.mult)
            p.ts(v, R1[0:1, 0:tw], 1.0 / 2048.0, ALU.mult)
            mm_r = rows[0:1, 512:512 + tw]
            p.tt(mm_r, m, m, ALU.mult)
            p.tt(v, v, mm_r, ALU.subtract)
            p.ts(v, v, 1e-5, ALU.add)
            p.act(v, v, AF.Ln)
            p.act(v, v, AF.Exp, scale=-0.5)
            p.tt(mm_r, m, v, ALU.mult)
            p.mm(P0[:, 0:tw], ones_row, v, start=True, stop=True)
            p.mm(P2[:, 0:tw], ones_row, mm_r, start=True, stop=True)
            rstd_b = xt
            p.scopy(sq[:, 0:tw], P0[:, 0:tw])
            p.scopy(obuf[:, 0:tw], P2[:, 0:tw])
            for c in range(DCH):
                cs = slice(c * tw, (c + 1) * tw)
                p.stt(xnb[:, cs], xt[:, cs], 0.0, sq[:, 0:tw], ALU.add, ALU.mult)
                p.tt(xnb[:, cs], xnb[:, cs], obuf[:, 0:tw], ALU.subtract)
            for c in range(DCH):
                p.dma(XN[c * 128:(c + 1) * 128, t0:t0 + tw], xnb[:, c * tw:(c + 1) * tw])

        WCOLS = {'fc': H, 'pr': D}
        def tern_stats(W, nrows, label):
            nch = nrows // 128
            per = 2560
            ncols = WCOLS[label]
            nslices = (ncols + per - 1) // per
            cnt = 0
            for c in range(nch):
                for sl in range(nslices):
                    c0 = sl * per
                    cw = min(per, ncols - c0)
                    p.dma(lbuf[:, 0:cw], W[c * 128:(c + 1) * 128, c0:c0 + cw])
                    p.act(s2[:, 0:cw], lbuf[:, 0:cw], AF.Abs)
                    p.rsum(cola[:, 4:5], s2[:, 0:cw])
                    p.mm(R0[0:1, 0:1], ones_col, cola[:, 4:5],
                         start=(cnt == 0), stop=(cnt == nch * nslices - 1))
                    cnt += 1
            tot = float(nrows) * float(ncols)
            p.ts(rows[0:1, 0:1], R0[0:1, 0:1], 1.0 / tot, ALU.mult, 1e-5, ALU.max)
            p.ts(rows[0:1, 1:2], rows[0:1, 0:1], 0.7, ALU.mult)
            p.mm(P0[:, 0:2], ones_row, rows[0:1, 0:2], start=True, stop=True)
            if label == 'fc':
                amc, thc = cola[:, 0:1], cola[:, 1:2]
            else:
                amc, thc = cola[:, 2:3], cola[:, 3:4]
            p.scopy(amc, P0[:, 0:1])
            p.scopy(thc, P0[:, 1:2])
            return amc, thc

        am_fc, th_fc = tern_stats(WFC, D, 'fc')
        am_pr, th_pr = tern_stats(WPR, H, 'pr')

        def tern_load(W, rch, c0, cw, dst_ap, amc, thc):
            p.dma(lbuf[:, 0:cw], W[rch * 128:(rch + 1) * 128, c0:c0 + cw])
            p.act(s2[:, 0:cw], lbuf[:, 0:cw], AF.Abs)
            p.act(s3[:, 0:cw], lbuf[:, 0:cw], AF.Sign)
            p.ts(s2[:, 0:cw], s2[:, 0:cw], thc, ALU.is_gt, amc, ALU.mult)
            p.tt(dst_ap, s3[:, 0:cw], s2[:, 0:cw], ALU.mult)

        QH = 1280
        for qi in range(4):
            h0 = qi * QH
            for c in range(DCH):
                tern_load(WFC, c, h0, QH, wbuf[:, c * QH:(c + 1) * QH], am_fc, th_fc)
            for ti in range(8):
                t0 = ti * 512
                tw = 512
                for c in range(DCH):
                    p.dma(xnb[:, c * tw:(c + 1) * tw], XN[c * 128:(c + 1) * 128, t0:t0 + tw])
                for j in range(QH // 128):
                    for c in range(DCH):
                        p.mm(P0[:, 0:tw], wbuf[:, c * QH + j * 128: c * QH + (j + 1) * 128],
                             xnb[:, c * tw:(c + 1) * tw], start=(c == 0), stop=(c == DCH - 1))
                    p.act(obuf[:, 0:tw], P0[:, 0:tw], AF.Relu)
                    p.tt(hhb[:, 0:tw], obuf[:, 0:tw], obuf[:, 0:tw], ALU.mult)
                    p.dma(HH[h0 + j * 128: h0 + (j + 1) * 128, t0:t0 + tw], hhb[:, 0:tw])

        QF = 512
        HCH = H // 128
        for qi in range(4):
            f0 = qi * QF
            for hc in range(HCH):
                tern_load(WPR, hc, f0, QF, wbuf[:, hc * QF:(hc + 1) * QF], am_pr, th_pr)
            for ti in range(16):
                t0 = ti * 256
                tw = 256
                for hc in range(HCH):
                    p.dma(hhb[:, hc * tw:(hc + 1) * tw], HH[hc * 128:(hc + 1) * 128, t0:t0 + tw])
                for j in range(QF // 128):
                    fr = f0 + j * 128
                    p.dma(xt[:, 0:tw], YIN[fr:fr + 128, t0:t0 + tw])
                    for hc in range(HCH):
                        p.mm(P0[:, 0:tw], wbuf[:, hc * QF + j * 128: hc * QF + (j + 1) * 128],
                             hhb[:, hc * tw:(hc + 1) * tw], start=(hc == 0), stop=(hc == HCH - 1))
                    p.tt(obuf[:, 0:tw], P0[:, 0:tw], xt[:, 0:tw], ALU.add)
                    p.dma(YOUT[fr:fr + 128, t0:t0 + tw], obuf[:, 0:tw])

        p.emit(nc, blk, S)
    return nc



def _shard_cols(a):
    return [np.ascontiguousarray(a[:, c * TPC:(c + 1) * TPC]) for c in range(NCORES)]


def run_l0ffn(y1T, w_fc, w_proj):
    nc = build_l0ffn()
    ones = np.ones((128, 128), np.float32)
    wfcT = np.ascontiguousarray(w_fc.T)
    wprT = np.ascontiguousarray(w_proj.T)
    in_maps = [{"yin": s, "wfc": wfcT, "wpr": wprT, "const": ones}
               for s in _shard_cols(y1T)]
    res = run_bass_kernel_spmd(nc, in_maps, core_ids=list(range(NCORES)))
    return np.concatenate([r["yout"] for r in res.results], axis=1)


def kernel(L0_attn_x, L0_attn_slope_rate, L0_attn_W_qkvu, L0_attn_W_out,
           L0_ffn_w_fc, L0_ffn_w_proj, L1_attn_W_qkv, L1_attn_W_o,
           L1_ffn_w_gate, L1_ffn_w_shared_up, L1_ffn_w_shared_down,
           L1_ffn_w_up, L1_ffn_w_down):
    x = np.asarray(L0_attn_x, dtype=np.float32)
    xT_shards = [np.ascontiguousarray(x[b].T) for b in range(NCORES)]
    y1T = run_l0attn(xT_shards, L0_attn_slope_rate, L0_attn_W_qkvu, L0_attn_W_out)
    y2T = run_l0ffn(y1T, L0_ffn_w_fc, L0_ffn_w_proj)
    y2_shards = [np.ascontiguousarray(y2T[:, c * TPC:(c + 1) * TPC])
                 for c in range(NCORES)]
    y3T = run_l1attn(y2_shards, L1_attn_W_qkv, L1_attn_W_o)
    y3T = np.ascontiguousarray(y3T)
    outr, dst6, w6 = run_moe(y3T, L1_ffn_w_gate, L1_ffn_w_up, L1_ffn_w_down)
    y3_shards = [np.ascontiguousarray(y3T[:, c * TPC:(c + 1) * TPC])
                 for c in range(NCORES)]
    y4T = run_comb(y3_shards, outr, dst6, w6,
                   L1_ffn_w_shared_up, L1_ffn_w_shared_down)
    out = np.ascontiguousarray(y4T.T).reshape(8, 4096, 2048).astype(np.float32)
    return out



def build_l0attn():
    nc = bass.Bass("TRN2", target_bir_lowering=False)
    XIN = nc.dram_tensor("xin", [D, TPC], F32, kind="ExternalInput").ap()
    SLOPE = nc.dram_tensor("slope", [1, 16], F32, kind="ExternalInput").ap()
    WQT = nc.dram_tensor("wqt", [D, 8192], F32, kind="ExternalInput").ap()
    WOT = nc.dram_tensor("wot", [D, D], F32, kind="ExternalInput").ap()
    CONST = nc.dram_tensor("const", [128, 128], F32, kind="ExternalInput").ap()
    ARRC = nc.dram_tensor("arrc", [128, 256], F32, kind="ExternalInput").ap()
    KIDC = nc.dram_tensor("kidc", [128, 2], F32, kind="ExternalInput").ap()
    MTC = nc.dram_tensor("mtc", [128, 512], F32, kind="ExternalInput").ap()
    EYEB = nc.dram_tensor("eyeb", [128, 128], BF16, kind="ExternalInput").ap()
    YOUT = nc.dram_tensor("yout", [D, TPC], F32, kind="ExternalOutput").ap()

    XNB_D = nc.dram_tensor("xnb_d", [D, TPC], BF16, kind="Internal").ap()
    QT_D = nc.dram_tensor("qt_d", [D, TPC], BF16, kind="Internal").ap()
    KT_D = nc.dram_tensor("kt_d", [D, TPC], BF16, kind="Internal").ap()
    UT_D = nc.dram_tensor("ut_d", [D, TPC], BF16, kind="Internal").ap()
    VT_D = nc.dram_tensor("vt_d", [TPC, D], BF16, kind="Internal").ap()
    O2_D = nc.dram_tensor("o2_d", [D, TPC], BF16, kind="Internal").ap()

    p = Prog()
    with ExitStack() as _st:
        ec = _st.enter_context
        ones = ec(nc.sbuf_tensor("ones", [128, 128], F32))
        eyeb = ec(nc.sbuf_tensor("eyeb_sb", [128, 128], BF16))
        arr = ec(nc.sbuf_tensor("arr", [128, 256], F32))
        kid = ec(nc.sbuf_tensor("kid", [128, 2], F32))
        mt = ec(nc.sbuf_tensor("mt", [128, 512], F32))
        slp = ec(nc.sbuf_tensor("slp", [1, 16], F32))
        scol = ec(nc.sbuf_tensor("scol", [128, 16], F32))
        bdc = ec(nc.sbuf_tensor("bdc", [128, 16], F32))
        kdc = ec(nc.sbuf_tensor("kdc", [128, 32], F32))
        qdb = ec(nc.sbuf_tensor("qdb", [128, 4096], BF16))
        dtb = ec(nc.sbuf_tensor("dtb", [128, 8192], BF16))
        dtmp = ec(nc.sbuf_tensor("dtmp", [128, 512], F32))
        AR1 = ec(nc.sbuf_tensor("AR1", [128, 8192], F32))
        AR2 = ec(nc.sbuf_tensor("AR2", [128, 16384], BF16))
        AR3 = ec(nc.sbuf_tensor("AR3", [128, 8192], BF16))
        lbuf = ec(nc.sbuf_tensor("lbuf", [128, 512], F32))
        sq = ec(nc.sbuf_tensor("sq", [128, 512], F32))
        ob2 = ec(nc.sbuf_tensor("ob2", [128, 512], F32))
        rows = ec(nc.sbuf_tensor("rows", [1, 2048], F32))
        rows2 = ec(nc.sbuf_tensor("rows2", [1, 2048], F32))
        KVT = ec(nc.sbuf_tensor("KVT", [128, 2048], F32))
        P0 = ec(nc.psum_tensor("P0", [128, 512], F32))
        P1 = ec(nc.psum_tensor("P1", [128, 512], F32))
        P2 = ec(nc.psum_tensor("P2", [128, 512], F32))
        R0 = ec(nc.psum_tensor("R0", [1, 512], F32))
        R1 = ec(nc.psum_tensor("R1", [1, 512], F32))
        PTB = ec(nc.psum_tensor("PTB", [128, 128], BF16))
        S = ec(nc.semaphore("S"))
        blk = ec(nc.Block())

        ones_col = ones[:, 0:1]
        ones_row = ones[0:1, :]

        p.dma(ones[:, :], CONST[:, :])
        p.dma(eyeb[:, :], EYEB[:, :])
        p.dma(arr[:, :], ARRC[:, :])
        p.dma(kid[:, :], KIDC[:, :])
        p.dma(mt[:, :], MTC[:, :])
        p.dma(slp[:, :], SLOPE[:, :])
        p.memset(KVT[:, :], 0.0)

        for h in range(16):
            p.mm(P2[:, 0:1], ones_row, slp[0:1, h:h + 1], start=True, stop=True)
            p.scopy(scol[:, h:h + 1], P2[:, 0:1])
            sc = scol[:, h:h + 1]
            p.ts(dtmp[:, 0:256], arr[:, :], sc, ALU.mult, -1.0, ALU.mult)
            p.act(qdb[:, h * 256:(h + 1) * 256], dtmp[:, 0:256], AF.Exp)
            p.ts(dtmp[:, 256:258], kid[:, :], sc, ALU.mult, -1.0, ALU.mult)
            p.act(kdc[:, 2 * h:2 * h + 2], dtmp[:, 256:258], AF.Exp)
            p.ts(dtmp[:, 258:259], sc, -256.0, ALU.mult)
            p.act(bdc[:, h:h + 1], dtmp[:, 258:259], AF.Exp)
            p.ts(dtmp[:, 0:512], mt[:, :], sc, ALU.mult, -1.0, ALU.mult)
            p.act(dtb[:, h * 512:(h + 1) * 512], dtmp[:, 0:512], AF.Exp)

        xt = AR1
        xnb = AR2
        for ti in range(8):
            t0 = ti * 512
            tw = 512
            for c in range(DCH):
                p.dma(xt[:, c * tw:(c + 1) * tw], XIN[c * 128:(c + 1) * 128, t0:t0 + tw])
            for c in range(DCH):
                cs = slice(c * tw, (c + 1) * tw)
                p.mm(R0[0:1, 0:tw], ones_col, xt[:, cs], start=(c == 0), stop=(c == DCH - 1))
                p.act(sq[:, 0:tw], xt[:, cs], AF.Square)
                p.mm(R1[0:1, 0:tw], ones_col, sq[:, 0:tw], start=(c == 0), stop=(c == DCH - 1))
            m = rows[0:1, 0:tw]
            v = rows2[0:1, 0:tw]
            p.ts(m, R0[0:1, 0:tw], 1.0 / 2048.0, ALU.mult)
            p.ts(v, R1[0:1, 0:tw], 1.0 / 2048.0, ALU.mult)
            mm_r = rows[0:1, 512:512 + tw]
            p.tt(mm_r, m, m, ALU.mult)
            p.tt(v, v, mm_r, ALU.subtract)
            p.ts(v, v, 1e-5, ALU.add)
            p.act(v, v, AF.Ln)
            p.act(v, v, AF.Exp, scale=-0.5)
            p.tt(mm_r, m, v, ALU.mult)
            p.mm(P0[:, 0:tw], ones_row, v, start=True, stop=True)
            p.mm(P2[:, 0:tw], ones_row, mm_r, start=True, stop=True)
            p.scopy(sq[:, 0:tw], P0[:, 0:tw])
            p.scopy(ob2[:, 0:tw], P2[:, 0:tw])
            for c in range(DCH):
                cs = slice(c * tw, (c + 1) * tw)
                p.stt(xnb[:, cs], xt[:, cs], 0.0, sq[:, 0:tw], ALU.add, ALU.mult)
                p.tt(xnb[:, cs], xnb[:, cs], ob2[:, 0:tw], ALU.subtract)
            for c in range(DCH):
                p.dma(XNB_D[c * 128:(c + 1) * 128, t0:t0 + tw], xnb[:, c * tw:(c + 1) * tw])

        wbf = AR3
        slices = []
        for i4 in range(4):
            slices.append((i4 * 512, QT_D, True))
        for i4 in range(4):
            slices.append((2048 + i4 * 512, KT_D, True))
        for i4 in range(4):
            slices.append((6144 + i4 * 512, UT_D, False))
        for (f0, DEST, dosilu) in slices:
            fl = f0 % 2048
            for c in range(DCH):
                p.dma(lbuf[:, :], WQT[c * 128:(c + 1) * 128, f0:f0 + 512])
                p.copy(wbf[:, c * 512:(c + 1) * 512], lbuf[:, :])
            for ti in range(8):
                t0 = ti * 512
                for c in range(DCH):
                    p.dma(xnb[:, c * 512:(c + 1) * 512],
                          XNB_D[c * 128:(c + 1) * 128, t0:t0 + 512])
                for j in range(4):
                    for c in range(DCH):
                        p.mm(P0[:, :], wbf[:, c * 512 + j * 128: c * 512 + (j + 1) * 128],
                             xnb[:, c * 512:(c + 1) * 512], start=(c == 0), stop=(c == DCH - 1))
                    dst_b = AR2[:, 8192:8704]
                    if dosilu:
                        p.act(dst_b, P0[:, :], AF.Silu)
                    else:
                        p.scopy(dst_b, P0[:, :])
                    p.dma(DEST[fl + j * 128: fl + (j + 1) * 128, t0:t0 + 512], dst_b)

        for es in range(4):
            f0 = 4096 + es * 512
            for c in range(DCH):
                p.dma(lbuf[:, :], WQT[c * 128:(c + 1) * 128, f0:f0 + 512])
                p.copy(wbf[:, c * 512:(c + 1) * 512], lbuf[:, :])
            for ti in range(8):
                t0 = ti * 512
                for c in range(DCH):
                    p.dma(xnb[:, c * 512:(c + 1) * 512],
                          XNB_D[c * 128:(c + 1) * 128, t0:t0 + 512])
                for tq in range(4):
                    for c in range(DCH):
                        p.mm(P0[:, :], xnb[:, c * 512 + tq * 128: c * 512 + (tq + 1) * 128],
                             wbf[:, c * 512:(c + 1) * 512], start=(c == 0), stop=(c == DCH - 1))
                    dst_b = AR2[:, 8192:8704]
                    p.scopy(dst_b, P0[:, :])
                    p.dma(VT_D[t0 + tq * 128: t0 + (tq + 1) * 128, es * 512:(es + 1) * 512], dst_b)

        qt = lambda h: AR2[:, h * 256:(h + 1) * 256]
        kt = lambda h: AR2[:, 4096 + h * 256:4096 + (h + 1) * 256]
        ubf = lambda c: AR2[:, 8192 + c * 256:8192 + (c + 1) * 256]
        obf = lambda c: AR2[:, 12288 + c * 256:12288 + (c + 1) * 256]
        vtok = lambda j2, h: AR3[:, j2 * 2048 + h * 128: j2 * 2048 + (h + 1) * 128]
        s_bf = AR3[:, 4096:4608]
        qp = AR3[:, 4608:4864]
        ktok = AR3[:, 4864:5120]
        kvt_bf = AR3[:, 5120:5248]
        kvbf = lambda h: AR3[:, 5248 + h * 128:5248 + (h + 1) * 128]
        OB = lambda c: AR1[:, c * 256:(c + 1) * 256]
        tmpf = AR1[:, 4096:4352]

        for blki in range(16):
            t0 = blki * 256
            for c in range(DCH):
                p.dma(AR2[:, c * 256:(c + 1) * 256], QT_D[c * 128:(c + 1) * 128, t0:t0 + 256])
                p.dma(AR2[:, 4096 + c * 256:4096 + (c + 1) * 256],
                      KT_D[c * 128:(c + 1) * 128, t0:t0 + 256])
                p.dma(AR2[:, 8192 + c * 256:8192 + (c + 1) * 256],
                      UT_D[c * 128:(c + 1) * 128, t0:t0 + 256])
            for j2 in range(2):
                p.dma(AR3[:, j2 * 2048:(j2 + 1) * 2048],
                      VT_D[t0 + j2 * 128:t0 + (j2 + 1) * 128, :])
            for h in range(16):
                p.mm(P0[:, 0:256], kt(h)[:, 0:128], qt(h), start=True, stop=True)
                p.mm(P1[:, 0:256], kt(h)[:, 128:256], qt(h), start=True, stop=True)
                p.copy(s_bf[:, 0:256], P0[:, 0:256])
                p.copy(s_bf[:, 256:512], P1[:, 0:256])
                p.tt(s_bf[:, 0:256], s_bf[:, 0:256], dtb[:, h * 512:h * 512 + 256], ALU.mult)
                p.tt(s_bf[:, 256:512], s_bf[:, 256:512], dtb[:, h * 512 + 256:h * 512 + 512], ALU.mult)
                p.tt(qp, qt(h), qdb[:, h * 256:(h + 1) * 256], ALU.mult)
                first = True
                if blki > 0:
                    p.mm(P2[:, 0:256], kvbf(h), qp, start=True, stop=False)
                    first = False
                p.mm(P2[:, 0:256], vtok(0, h), s_bf[:, 0:256], start=first, stop=False)
                p.mm(P2[:, 0:256], vtok(1, h), s_bf[:, 256:512], start=False, stop=True)
                p.scopy(OB(h), P2[:, 0:256])
                if blki < 15:
                    for j2 in range(2):
                        p.tr(PTB[:, :], kt(h)[:, j2 * 128:(j2 + 1) * 128], eyeb[:, :])
                        p.ts(ktok[:, j2 * 128:(j2 + 1) * 128], PTB[:, :],
                             kdc[:, 2 * h + j2:2 * h + j2 + 1], ALU.mult)
                    p.mm(P0[:, 0:128], vtok(0, h), ktok[:, 0:128], start=True, stop=False)
                    p.mm(P0[:, 0:128], vtok(1, h), ktok[:, 128:256], start=False, stop=True)
                    p.stt(KVT[:, h * 128:(h + 1) * 128], KVT[:, h * 128:(h + 1) * 128],
                          bdc[:, h:h + 1], P0[:, 0:128], ALU.mult, ALU.add)
                    p.copy(kvt_bf, KVT[:, h * 128:(h + 1) * 128])
                    p.tr(PTB[:, :], kvt_bf, eyeb[:, :])
                    p.copy(kvbf(h), PTB[:, :])
            for c in range(DCH):
                p.act(sq[:, 0:256], OB(c), AF.Square)
                p.mm(R0[0:1, 0:256], ones_col, sq[:, 0:256], start=(c == 0), stop=(c == DCH - 1))
            rr = rows[0:1, 0:256]
            p.ts(rr, R0[0:1, 0:256], 1.0 / 2048.0, ALU.mult, 1e-6, ALU.add)
            p.act(rr, rr, AF.Ln)
            p.act(rr, rr, AF.Exp, scale=-0.5)
            p.mm(P0[:, 0:256], ones_row, rr, start=True, stop=True)
            p.scopy(ob2[:, 0:256], P0[:, 0:256])
            for c in range(DCH):
                p.tt(s_bf[:, 0:256], OB(c), ob2[:, 0:256], ALU.mult)
                p.tt(obf(c), s_bf[:, 0:256], ubf(c), ALU.mult)
            for c in range(DCH):
                p.dma(O2_D[c * 128:(c + 1) * 128, t0:t0 + 256], obf(c))

        for fs in range(4):
            f0 = fs * 512
            for c in range(DCH):
                p.dma(lbuf[:, :], WOT[c * 128:(c + 1) * 128, f0:f0 + 512])
                p.copy(wbf[:, c * 512:(c + 1) * 512], lbuf[:, :])
            for ti in range(8):
                t0 = ti * 512
                for c in range(DCH):
                    p.dma(xnb[:, c * 512:(c + 1) * 512],
                          O2_D[c * 128:(c + 1) * 128, t0:t0 + 512])
                for j in range(4):
                    for c in range(DCH):
                        p.mm(P0[:, :], wbf[:, c * 512 + j * 128: c * 512 + (j + 1) * 128],
                             xnb[:, c * 512:(c + 1) * 512], start=(c == 0), stop=(c == DCH - 1))
                    p.dma(lbuf[:, :], XIN[f0 + j * 128: f0 + (j + 1) * 128, t0:t0 + 512])
                    p.tt(sq[:, 0:512], P0[:, :], lbuf[:, :], ALU.add)
                    p.dma(YOUT[f0 + j * 128: f0 + (j + 1) * 128, t0:t0 + 512], sq[:, 0:512])

        p.emit(nc, blk, S)
    return nc


def _l0attn_consts():
    arr = np.broadcast_to(np.arange(1, 257, dtype=np.float32)[None, :], (128, 256)).copy()
    kid = np.zeros((128, 2), np.float32)
    for c in range(2):
        kid[:, c] = 255.0 - (c * 128 + np.arange(128))
    mt = np.zeros((128, 512), np.float32)
    for j2 in range(2):
        for pp in range(128):
            t2 = j2 * 128 + pp
            for i in range(256):
                d = i - t2
                mt[pp, j2 * 256 + i] = d if d >= 0 else 1e9
    eyeb = np.eye(128).astype(BF16NP)
    return arr, kid, mt, eyeb


def run_l0attn(xT_shards, slope, W_qkvu, W_out):
    nc = build_l0attn()
    ones = np.ones((128, 128), np.float32)
    arr, kid, mt, eyeb = _l0attn_consts()
    wqt = np.ascontiguousarray(W_qkvu.T)
    wot = np.ascontiguousarray(W_out.T)
    slp = np.ascontiguousarray(slope.reshape(1, 16))
    in_maps = [{"xin": s, "slope": slp, "wqt": wqt, "wot": wot, "const": ones,
                "arrc": arr, "kidc": kid, "mtc": mt, "eyeb": eyeb}
               for s in xT_shards]
    res = run_bass_kernel_spmd(nc, in_maps, core_ids=list(range(NCORES)))
    return np.concatenate([r["yout"] for r in res.results], axis=1)



def build_l1attn():
    nc = bass.Bass("TRN2", target_bir_lowering=False)
    XIN = nc.dram_tensor("xin", [D, TPC], F32, kind="ExternalInput").ap()
    WQKT = nc.dram_tensor("wqkt", [D, 6144], F32, kind="ExternalInput").ap()
    WOT = nc.dram_tensor("wot", [D, D], F32, kind="ExternalInput").ap()
    CONST = nc.dram_tensor("const", [128, 128], F32, kind="ExternalInput").ap()
    PERMT = nc.dram_tensor("permt", [128, 128], BF16, kind="ExternalInput").ap()
    COSB = nc.dram_tensor("cosb", [128, TPC], BF16, kind="ExternalInput").ap()
    SINB = nc.dram_tensor("sinb", [128, TPC], BF16, kind="ExternalInput").ap()
    M01 = nc.dram_tensor("m01", [128, 2048], BF16, kind="ExternalInput").ap()
    YOUT = nc.dram_tensor("yout", [D, TPC], F32, kind="ExternalOutput").ap()

    XNB_D = nc.dram_tensor("xnb_d", [D, TPC], BF16, kind="Internal").ap()
    QR_D = nc.dram_tensor("qr_d", [D, TPC], BF16, kind="Internal").ap()
    KR_D = nc.dram_tensor("kr_d", [D, TPC], BF16, kind="Internal").ap()
    VT_D = nc.dram_tensor("vt_d", [TPC, D], BF16, kind="Internal").ap()
    AO_D = nc.dram_tensor("ao_d", [D, TPC], BF16, kind="Internal").ap()

    p = Prog()
    with ExitStack() as _st:
        ec = _st.enter_context
        ones = ec(nc.sbuf_tensor("ones", [128, 128], F32))
        permt = ec(nc.sbuf_tensor("permt_sb", [128, 128], BF16))
        cosb = ec(nc.sbuf_tensor("cosb_sb", [128, 512], BF16))
        sinb = ec(nc.sbuf_tensor("sinb_sb", [128, 512], BF16))
        m01 = ec(nc.sbuf_tensor("m01_sb", [128, 2048], BF16))
        AR1 = ec(nc.sbuf_tensor("AR1", [128, 8192], F32))
        AR2 = ec(nc.sbuf_tensor("AR2", [128, 16384], BF16))
        AR3 = ec(nc.sbuf_tensor("AR3", [128, 8192], BF16))
        PB = ec(nc.sbuf_tensor("PB", [128, 16384], BF16))
        lbuf = ec(nc.sbuf_tensor("lbuf", [128, 512], F32))
        sq = ec(nc.sbuf_tensor("sq", [128, 512], F32))
        ob2 = ec(nc.sbuf_tensor("ob2", [128, 512], F32))
        dpart = ec(nc.sbuf_tensor("dpart", [128, 512], F32))
        rows = ec(nc.sbuf_tensor("rows", [1, 2048], F32))
        rows2 = ec(nc.sbuf_tensor("rows2", [1, 2048], F32))
        P0 = ec(nc.psum_tensor("P0", [128, 512], F32))
        P1 = ec(nc.psum_tensor("P1", [128, 512], F32))
        P2 = ec(nc.psum_tensor("P2", [128, 512], F32))
        R0 = ec(nc.psum_tensor("R0", [1, 512], F32))
        R1 = ec(nc.psum_tensor("R1", [1, 512], F32))
        S = ec(nc.semaphore("S"))
        blk = ec(nc.Block())

        ones_col = ones[:, 0:1]
        ones_row = ones[0:1, :]
        p.dma(ones[:, :], CONST[:, :])
        p.dma(permt[:, :], PERMT[:, :])
        p.dma(m01[:, :], M01[:, :])

        xt = AR1
        xnb = AR2

        for ti in range(8):
            t0 = ti * 512
            tw = 512
            for c in range(DCH):
                p.dma(xt[:, c * tw:(c + 1) * tw], XIN[c * 128:(c + 1) * 128, t0:t0 + tw])
            for c in range(DCH):
                cs = slice(c * tw, (c + 1) * tw)
                p.mm(R0[0:1, 0:tw], ones_col, xt[:, cs], start=(c == 0), stop=(c == DCH - 1))
                p.act(sq[:, 0:tw], xt[:, cs], AF.Square)
                p.mm(R1[0:1, 0:tw], ones_col, sq[:, 0:tw], start=(c == 0), stop=(c == DCH - 1))
            m = rows[0:1, 0:tw]
            v = rows2[0:1, 0:tw]
            p.ts(m, R0[0:1, 0:tw], 1.0 / 2048.0, ALU.mult)
            p.ts(v, R1[0:1, 0:tw], 1.0 / 2048.0, ALU.mult)
            mm_r = rows[0:1, 512:512 + tw]
            p.tt(mm_r, m, m, ALU.mult)
            p.tt(v, v, mm_r, ALU.subtract)
            p.ts(v, v, 1e-5, ALU.add)
            p.act(v, v, AF.Ln)
            p.act(v, v, AF.Exp, scale=-0.5)
            p.tt(mm_r, m, v, ALU.mult)
            p.mm(P0[:, 0:tw], ones_row, v, start=True, stop=True)
            p.mm(P2[:, 0:tw], ones_row, mm_r, start=True, stop=True)
            p.scopy(sq[:, 0:tw], P0[:, 0:tw])
            p.scopy(ob2[:, 0:tw], P2[:, 0:tw])
            for c in range(DCH):
                cs = slice(c * tw, (c + 1) * tw)
                p.stt(xnb[:, cs], xt[:, cs], 0.0, sq[:, 0:tw], ALU.add, ALU.mult)
                p.tt(xnb[:, cs], xnb[:, cs], ob2[:, 0:tw], ALU.subtract)
            for c in range(DCH):
                p.dma(XNB_D[c * 128:(c + 1) * 128, t0:t0 + tw], xnb[:, c * tw:(c + 1) * tw])

        wbf = AR3
        qsb = AR2[:, 8192:8704]
        rotb = AR2[:, 8704:9216]
        ab = AR2[:, 9216:9728]
        bb = AR2[:, 9728:10240]
        for si in range(8):
            f0 = si * 512
            DEST = QR_D if si < 4 else KR_D
            fl = f0 % 2048
            for c in range(DCH):
                p.dma(lbuf[:, :], WQKT[c * 128:(c + 1) * 128, f0:f0 + 512])
                p.copy(wbf[:, c * 512:(c + 1) * 512], lbuf[:, :])
            for ti in range(8):
                t0 = ti * 512
                p.dma(cosb[:, :], COSB[:, t0:t0 + 512])
                p.dma(sinb[:, :], SINB[:, t0:t0 + 512])
                for c in range(DCH):
                    p.dma(xnb[:, c * 512:(c + 1) * 512],
                          XNB_D[c * 128:(c + 1) * 128, t0:t0 + 512])
                for j in range(4):
                    for c in range(DCH):
                        p.mm(P0[:, :], wbf[:, c * 512 + j * 128: c * 512 + (j + 1) * 128],
                             xnb[:, c * 512:(c + 1) * 512], start=(c == 0), stop=(c == DCH - 1))
                    p.copy(qsb, P0[:, :])
                    p.mm(P1[:, :], permt[:, :], qsb, start=True, stop=True)
                    p.copy(rotb, P1[:, :])
                    p.tt(ab, qsb, cosb[:, :], ALU.mult)
                    p.tt(bb, rotb, sinb[:, :], ALU.mult)
                    p.tt(ab, ab, bb, ALU.add)
                    p.dma(DEST[fl + j * 128: fl + (j + 1) * 128, t0:t0 + 512], ab)

        for es in range(4):
            f0 = 4096 + es * 512
            for c in range(DCH):
                p.dma(lbuf[:, :], WQKT[c * 128:(c + 1) * 128, f0:f0 + 512])
                p.copy(wbf[:, c * 512:(c + 1) * 512], lbuf[:, :])
            for ti in range(8):
                t0 = ti * 512
                for c in range(DCH):
                    p.dma(xnb[:, c * 512:(c + 1) * 512],
                          XNB_D[c * 128:(c + 1) * 128, t0:t0 + 512])
                for tq in range(4):
                    for c in range(DCH):
                        p.mm(P0[:, :], xnb[:, c * 512 + tq * 128: c * 512 + (tq + 1) * 128],
                             wbf[:, c * 512:(c + 1) * 512], start=(c == 0), stop=(c == DCH - 1))
                    p.scopy(qsb, P0[:, :])
                    p.dma(VT_D[t0 + tq * 128: t0 + (tq + 1) * 128, es * 512:(es + 1) * 512], qsb)

        qt = AR2[:, 0:512]
        SC = 1.0 / np.sqrt(128.0)
        for h in range(16):
            for g in range(8):
                t0 = g * 512
                nch = 4 * (g + 1)
                p.dma(qt, QR_D[h * 128:(h + 1) * 128, t0:t0 + 512])
                kt = AR2[:, 512:512 + nch * 128]
                p.dma(kt, KR_D[h * 128:(h + 1) * 128, 0:nch * 128])
                for pc in range((nch + 7) // 8):
                    r0 = pc * 8 * 128
                    rn = min(8 * 128, nch * 128 - r0)
                    p.dma(AR2[:, 4608 + pc * 1024:4608 + pc * 1024 + rn].rearrange(
                              "p (c e) -> p c e", e=128),
                          VT_D[r0:r0 + rn, h * 128:(h + 1) * 128].rearrange(
                              "(c p) e -> p c e", p=128))
                for t2c in range(nch):
                    p.mm(P0[:, :], kt[:, t2c * 128:(t2c + 1) * 128], qt,
                         start=True, stop=True)
                    pslot = PB[:, t2c * 512:(t2c + 1) * 512]
                    p.act(pslot, P0[:, :], AF.Exp, scale=SC)
                    dj = t2c - 4 * g
                    if dj >= 0:
                        p.tt(pslot, pslot, m01[:, dj * 512:(dj + 1) * 512], ALU.mult)
                    p.mm(P1[:, :], AR2[:, 4608 + t2c * 128:4608 + (t2c + 1) * 128],
                         pslot, start=(t2c == 0), stop=(t2c == nch - 1))
                pbv = PB[:, 0:nch * 512].rearrange("p (c t) -> p t c", c=nch)
                p.rsum(dpart[:, :], pbv)
                p.mm(R0[0:1, :], ones_col, dpart[:, :], start=True, stop=True)
                dr = rows[0:1, 0:512]
                p.recip(dr, R0[0:1, :])
                p.mm(P2[:, :], ones_row, dr, start=True, stop=True)
                p.scopy(ob2[:, :], P2[:, :])
                p.tt(sq[:, :], P1[:, :], ob2[:, :], ALU.mult)
                p.copy(qsb, sq[:, :])
                p.dma(AO_D[h * 128:(h + 1) * 128, t0:t0 + 512], qsb)

        for fs in range(4):
            f0 = fs * 512
            for c in range(DCH):
                p.dma(lbuf[:, :], WOT[c * 128:(c + 1) * 128, f0:f0 + 512])
                p.copy(wbf[:, c * 512:(c + 1) * 512], lbuf[:, :])
            for ti in range(8):
                t0 = ti * 512
                for c in range(DCH):
                    p.dma(xnb[:, c * 512:(c + 1) * 512],
                          AO_D[c * 128:(c + 1) * 128, t0:t0 + 512])
                for j in range(4):
                    for c in range(DCH):
                        p.mm(P0[:, :], wbf[:, c * 512 + j * 128: c * 512 + (j + 1) * 128],
                             xnb[:, c * 512:(c + 1) * 512], start=(c == 0), stop=(c == DCH - 1))
                    p.dma(lbuf[:, :], XIN[f0 + j * 128: f0 + (j + 1) * 128, t0:t0 + 512])
                    p.tt(sq[:, :], P0[:, :], lbuf[:, :], ALU.add)
                    p.dma(YOUT[f0 + j * 128: f0 + (j + 1) * 128, t0:t0 + 512], sq[:, :])

        p.emit(nc, blk, S)
    return nc


def _l1attn_consts():
    dk = 128
    inv = 1.0 / 10000.0 ** (np.arange(0, dk, 2, dtype=np.float64) / dk)
    ang = np.arange(TPC, dtype=np.float64)[:, None] * inv[None, :]
    ang = np.repeat(ang, 2, axis=-1)
    cosb = np.cos(ang).T.astype(BF16NP).copy()
    sinb = np.sin(ang).T.astype(BF16NP).copy()
    permt = np.zeros((128, 128), np.float32)
    for i in range(64):
        permt[2 * i + 1, 2 * i] = -1.0
        permt[2 * i, 2 * i + 1] = 1.0
    permt = permt.astype(BF16NP)
    m01 = np.zeros((128, 2048), np.float32)
    for dj in range(4):
        for pp in range(128):
            for jj in range(512):
                if jj >= dj * 128 + pp:
                    m01[pp, dj * 512 + jj] = 1.0
    m01 = m01.astype(BF16NP)
    return cosb, sinb, permt, m01


def run_l1attn(y2T_shards, W_qkv, W_o):
    nc = build_l1attn()
    ones = np.ones((128, 128), np.float32)
    cosb, sinb, permt, m01 = _l1attn_consts()
    wqkt = np.ascontiguousarray(W_qkv.T)
    wot = np.ascontiguousarray(W_o.T)
    in_maps = [{"xin": s, "wqkt": wqkt, "wot": wot, "const": ones,
                "permt": permt, "cosb": cosb, "sinb": sinb, "m01": m01}
               for s in y2T_shards]
    res = run_bass_kernel_spmd(nc, in_maps, core_ids=list(range(NCORES)))
    return np.concatenate([r["yout"] for r in res.results], axis=1)



NTOK = 32768

def build_moe():
    nc = bass.Bass("TRN2", target_bir_lowering=False)
    Y3F = nc.dram_tensor("y3f", [D, NTOK], F32, kind="ExternalInput").ap()
    WG = nc.dram_tensor("wg", [D, 64], F32, kind="ExternalInput").ap()
    WUP = nc.dram_tensor("wup", [8, D, 512], F32, kind="ExternalInput").ap()
    WDN = nc.dram_tensor("wdn", [8, 256, D], F32, kind="ExternalInput").ap()
    CONST = nc.dram_tensor("const", [128, 128], F32, kind="ExternalInput").ap()
    LT = nc.dram_tensor("lt", [128, 128], F32, kind="ExternalInput").ap()
    EB = nc.dram_tensor("eb", [128, 64], F32, kind="ExternalInput").ap()
    WB = nc.dram_tensor("wb", [128, 64], F32, kind="ExternalInput").ap()
    EYEB = nc.dram_tensor("eyeb", [128, 128], BF16, kind="ExternalInput").ap()
    SHIFT = nc.dram_tensor("shift", [1, 1], F32, kind="ExternalInput").ap()
    OUTR = nc.dram_tensor("outr", [8192, D], BF16, kind="ExternalOutput").ap()
    DSTO = nc.dram_tensor("dsto", [NTOK, 8], F32, kind="ExternalOutput").ap()
    W6O = nc.dram_tensor("w6o", [NTOK, 8], F32, kind="ExternalOutput").ap()
    XNB_D = nc.dram_tensor("xnb_d", [D, NTOK], BF16, kind="Internal").ap()
    XDISP = nc.dram_tensor("xdisp", [8192, D], BF16, kind="Internal").ap()

    p = Prog()
    with ExitStack() as _st:
        ec = _st.enter_context
        ones = ec(nc.sbuf_tensor("ones", [128, 128], F32))
        eyeb = ec(nc.sbuf_tensor("eyeb_sb", [128, 128], BF16))
        lt = ec(nc.sbuf_tensor("lt_sb", [128, 128], F32))
        eb = ec(nc.sbuf_tensor("eb_sb", [128, 64], F32))
        wb = ec(nc.sbuf_tensor("wb_sb", [128, 64], F32))
        shc = ec(nc.sbuf_tensor("shc", [128, 1], F32))
        wgb = ec(nc.sbuf_tensor("wgb", [128, 1024], BF16))
        AR1 = ec(nc.sbuf_tensor("AR1", [128, 8192], F32))
        xnb = ec(nc.sbuf_tensor("xnb_t", [128, 8192], BF16))
        xtok = ec(nc.sbuf_tensor("xtok", [128, 2048], BF16))
        lbuf = ec(nc.sbuf_tensor("lbuf", [128, 512], F32))
        sq = ec(nc.sbuf_tensor("sq", [128, 512], F32))
        ob2 = ec(nc.sbuf_tensor("ob2", [128, 512], F32))
        rows = ec(nc.sbuf_tensor("rows", [1, 2048], F32))
        rows2 = ec(nc.sbuf_tensor("rows2", [1, 2048], F32))
        sc = ec(nc.sbuf_tensor("sc", [128, 64], F32))
        m8 = ec(nc.sbuf_tensor("m8", [128, 8], F32))
        repl = ec(nc.sbuf_tensor("repl", [128, 64], F32))
        ssel = ec(nc.sbuf_tensor("ssel", [128, 64], F32))
        sel = ec(nc.sbuf_tensor("sel", [128, 64], F32))
        offs = ec(nc.sbuf_tensor("offs", [1, 64], F32))
        pk = ec(nc.sbuf_tensor("pk", [128, 64], F32))
        pw = ec(nc.sbuf_tensor("pw", [128, 64], F32))
        pk8 = ec(nc.sbuf_tensor("pk8", [128, 8], F32))
        pw8 = ec(nc.sbuf_tensor("pw8", [128, 8], F32))
        qf = ec(nc.sbuf_tensor("qf", [128, 8], F32))
        pi = ec(nc.sbuf_tensor("pi", [128, 8], F32))
        efl = ec(nc.sbuf_tensor("efl", [128, 8], F32))
        s6 = ec(nc.sbuf_tensor("s6", [128, 8], F32))
        w6 = ec(nc.sbuf_tensor("w6", [128, 8], F32))
        vg = ec(nc.sbuf_tensor("vg", [128, 8], F32))
        dst = ec(nc.sbuf_tensor("dst", [128, 8], F32))
        dl = ec(nc.sbuf_tensor("dl", [128, 8], F32))
        du = ec(nc.sbuf_tensor("du", [128, 8], U32))
        wsc = ec(nc.sbuf_tensor("wsc", [128, 1], F32))
        wr = ec(nc.sbuf_tensor("wr", [128, 1], F32))
        wub = ec(nc.sbuf_tensor("wub", [128, 8192], BF16))
        wdb = ec(nc.sbuf_tensor("wdb", [128, 4096], BF16))
        xeT = ec(nc.sbuf_tensor("xeT", [128, 16384], BF16))
        usb = ec(nc.sbuf_tensor("usb", [128, 2048], F32))
        gs = ec(nc.sbuf_tensor("gs", [128, 1024], F32))
        hbuf = ec(nc.sbuf_tensor("hbuf", [128, 2048], BF16))
        orow = ec(nc.sbuf_tensor("orow", [128, 2048], BF16))
        P0 = ec(nc.psum_tensor("P0", [128, 512], F32))
        P1 = ec(nc.psum_tensor("P1", [128, 512], F32))
        P2 = ec(nc.psum_tensor("P2", [128, 512], F32))
        R0 = ec(nc.psum_tensor("R0", [1, 512], F32))
        R1 = ec(nc.psum_tensor("R1", [1, 512], F32))
        PTB = ec(nc.psum_tensor("PTB", [128, 128], BF16))
        S = ec(nc.semaphore("S"))
        blk = ec(nc.Block())

        ones_col = ones[:, 0:1]
        ones_row = ones[0:1, :]
        p.dma(ones[:, :], CONST[:, :])
        p.dma(eyeb[:, :], EYEB[:, :])
        p.dma(lt[:, :], LT[:, :])
        p.dma(eb[:, :], EB[:, :])
        p.dma(wb[:, :], WB[:, :])
        p.memset(offs[:, :], 0.0)
        p.dma(rows[0:1, 0:1], SHIFT[:, :])
        p.mm(P2[:, 0:1], ones_row, rows[0:1, 0:1], start=True, stop=True)
        p.scopy(shc[:, :], P2[:, 0:1])
        for c in range(DCH):
            p.dma(lbuf[:, 0:64], WG[c * 128:(c + 1) * 128, :])
            p.copy(wgb[:, c * 64:(c + 1) * 64], lbuf[:, 0:64])
        p.memset(xtok[:, :], 0.0)
        for s0 in range(64):
            p.dma(XDISP[s0 * 128:(s0 + 1) * 128, :], xtok[:, :])

        xt = AR1
        for ti in range(64):
            t0 = ti * 512
            tw = 512
            for c in range(DCH):
                p.dma(xt[:, c * tw:(c + 1) * tw], Y3F[c * 128:(c + 1) * 128, t0:t0 + tw])
            for c in range(DCH):
                cs = slice(c * tw, (c + 1) * tw)
                p.mm(R0[0:1, 0:tw], ones_col, xt[:, cs], start=(c == 0), stop=(c == DCH - 1))
                p.act(sq[:, 0:tw], xt[:, cs], AF.Square)
                p.mm(R1[0:1, 0:tw], ones_col, sq[:, 0:tw], start=(c == 0), stop=(c == DCH - 1))
            m = rows[0:1, 0:tw]
            v = rows2[0:1, 0:tw]
            p.ts(m, R0[0:1, 0:tw], 1.0 / 2048.0, ALU.mult)
            p.ts(v, R1[0:1, 0:tw], 1.0 / 2048.0, ALU.mult)
            mm_r = rows[0:1, 512:512 + tw]
            p.tt(mm_r, m, m, ALU.mult)
            p.tt(v, v, mm_r, ALU.subtract)
            p.ts(v, v, 1e-5, ALU.add)
            p.act(v, v, AF.Ln)
            p.act(v, v, AF.Exp, scale=-0.5)
            p.tt(mm_r, m, v, ALU.mult)
            p.mm(P0[:, 0:tw], ones_row, v, start=True, stop=True)
            p.mm(P2[:, 0:tw], ones_row, mm_r, start=True, stop=True)
            p.scopy(sq[:, 0:tw], P0[:, 0:tw])
            p.scopy(ob2[:, 0:tw], P2[:, 0:tw])
            for c in range(DCH):
                cs = slice(c * tw, (c + 1) * tw)
                p.stt(xnb[:, cs], xt[:, cs], 0.0, sq[:, 0:tw], ALU.add, ALU.mult)
                p.tt(xnb[:, cs], xnb[:, cs], ob2[:, 0:tw], ALU.subtract)
            for c in range(DCH):
                p.dma(XNB_D[c * 128:(c + 1) * 128, t0:t0 + tw], xnb[:, c * tw:(c + 1) * tw])

        for ch in range(256):
            t0 = ch * 128
            for c in range(DCH):
                p.dma(xnb[:, c * 128:(c + 1) * 128],
                      XNB_D[c * 128:(c + 1) * 128, t0:t0 + 128])
            for c in range(DCH):
                p.mm(P0[:, 0:64], xnb[:, c * 128:(c + 1) * 128],
                     wgb[:, c * 64:(c + 1) * 64], start=(c == 0), stop=(c == DCH - 1))
            p.act(sc[:, :], P0[:, 0:64], AF.Sigmoid)
            p.max8(m8[:, :], sc[:, :])
            p.memset(m8[:, 6:8], 0.0)
            p.mrep(repl[:, :], m8[:, :], sc[:, :], 0.0)
            p.tt(ssel[:, :], sc[:, :], repl[:, :], ALU.subtract)
            p.rsum(wsc[:, :], ssel[:, :])
            p.ts(sel[:, :], ssel[:, :], 0.0, ALU.is_gt)
            p.mm(P1[:, 0:64], lt[:, :], sel[:, :], start=True, stop=False)
            p.mm(P1[:, 0:64], ones_row, offs[:, :], start=False, stop=True)
            p.mm(R0[0:1, 0:64], ones_col, sel[:, :], start=True, stop=True)
            p.tt(pk[:, :], P1[:, 0:64], eb[:, :], ALU.add)
            p.tt(pk[:, :], pk[:, :], sel[:, :], ALU.mult)
            p.tt(pw[:, :], ssel[:, :], wb[:, :], ALU.add)
            p.tt(pw[:, :], pw[:, :], sel[:, :], ALU.mult)
            p.tt(offs[:, :], offs[:, :], R0[0:1, 0:64], ALU.add)
            p.max8(pk8[:, :], pk[:, :])
            p.max8(pw8[:, :], pw[:, :])
            p.ts(qf[:, 0:6], pk8[:, 0:6], 1.0 / 65536.0, ALU.mult, -0.5, ALU.add)
            p.copy(du[:, 0:6], qf[:, 0:6])
            p.copy(qf[:, 0:6], du[:, 0:6])
            p.stt(pi[:, 0:6], qf[:, 0:6], -65536.0, pk8[:, 0:6], ALU.mult, ALU.add)
            p.ts(efl[:, 0:6], qf[:, 0:6], -1.0, ALU.mult, 63.0, ALU.add)
            p.stt(s6[:, 0:6], qf[:, 0:6], -4.0, pw8[:, 0:6], ALU.mult, ALU.add)
            p.recip(wr[:, :], wsc[:, :])
            p.ts(w6[:, 0:6], s6[:, 0:6], wr[:, :], ALU.mult)
            p.ts(vg[:, 0:6], pi[:, 0:6], 1024.0, ALU.is_gt)
            p.stt(dst[:, 0:6], efl[:, 0:6], 1024.0, pi[:, 0:6], ALU.mult, ALU.add)
            p.ts(dst[:, 0:6], dst[:, 0:6], -1.0, ALU.add)
            p.stt(dst[:, 0:6], vg[:, 0:6], 1e7, dst[:, 0:6], ALU.mult, ALU.add)
            p.dma(DSTO[t0:t0 + 128, 0:6], dst[:, 0:6])
            p.dma(W6O[t0:t0 + 128, 0:6], w6[:, 0:6])
            p.ts(dl[:, 0:6], dst[:, 0:6], shc[:, :], ALU.subtract)
            p.ts(vg[:, 0:6], dl[:, 0:6], -0.5, ALU.is_lt)
            p.stt(dl[:, 0:6], vg[:, 0:6], 1e7, dl[:, 0:6], ALU.mult, ALU.add)
            p.copy(du[:, 0:6], dl[:, 0:6])
            for c in range(DCH):
                p.tr(PTB[:, :], xnb[:, c * 128:(c + 1) * 128], eyeb[:, :])
                p.copy(xtok[:, c * 128:(c + 1) * 128], PTB[:, :])
            for r in range(6):
                p.idma_scatter(XDISP[:, :], xtok[:, :], du[:, r:r + 1], 8191)

        for el in range(8):
            for s0 in range(8):
                p.dma(xtok[:, :], XDISP[el * 1024 + s0 * 128: el * 1024 + (s0 + 1) * 128, :])
                for c in range(DCH):
                    p.tr(PTB[:, :], xtok[:, c * 128:(c + 1) * 128], eyeb[:, :])
                    p.copy(xeT[:, c * 1024 + s0 * 128: c * 1024 + (s0 + 1) * 128], PTB[:, :])
            for c in range(DCH):
                p.dma(lbuf[:, :], WUP[el, c * 128:(c + 1) * 128, :])
                p.copy(wub[:, c * 512:(c + 1) * 512], lbuf[:, :])
            for chh in range(2):
                c0 = chh * 512
                for hj in range(4):
                    for c in range(DCH):
                        p.mm(P0[:, :], wub[:, c * 512 + hj * 128: c * 512 + (hj + 1) * 128],
                             xeT[:, c * 1024 + c0: c * 1024 + c0 + 512],
                             start=(c == 0), stop=(c == DCH - 1))
                    p.scopy(usb[:, hj * 512:(hj + 1) * 512], P0[:, :])
                p.act(gs[:, :], usb[:, 0:1024], AF.Silu)
                p.tt(hbuf[:, 0 * 1024 + c0: 0 * 1024 + c0 + 512], gs[:, 0:512],
                     usb[:, 1024:1536], ALU.mult)
                p.tt(hbuf[:, 1 * 1024 + c0: 1 * 1024 + c0 + 512], gs[:, 512:1024],
                     usb[:, 1536:2048], ALU.mult)
            for j in range(2):
                for fq in range(4):
                    p.dma(lbuf[:, :], WDN[el, j * 128:(j + 1) * 128, fq * 512:(fq + 1) * 512])
                    p.copy(wdb[:, j * 2048 + fq * 512: j * 2048 + (fq + 1) * 512], lbuf[:, :])
            for cc in range(8):
                for fj in range(4):
                    for j in range(2):
                        p.mm(P0[:, :], hbuf[:, j * 1024 + cc * 128:(j * 1024 + (cc + 1) * 128)],
                             wdb[:, j * 2048 + fj * 512: j * 2048 + (fj + 1) * 512],
                             start=(j == 0), stop=(j == 1))
                    p.copy(orow[:, fj * 512:(fj + 1) * 512], P0[:, :])
                p.dma(OUTR[el * 1024 + cc * 128: el * 1024 + (cc + 1) * 128, :], orow[:, :])

        p.emit(nc, blk, S)
    return nc


def _moe_consts():
    lt = np.zeros((128, 128), np.float32)
    for j in range(128):
        lt[j, j:] = 1.0
    e = np.arange(64, dtype=np.float32)
    eb = np.broadcast_to(((63.0 - e) * 65536.0)[None, :], (128, 64)).copy()
    wbc = np.broadcast_to(((63.0 - e) * 4.0)[None, :], (128, 64)).copy()
    return lt, eb, wbc


def run_moe(y3T, w_gate, w_up, w_down):
    nc = build_moe()
    ones = np.ones((128, 128), np.float32)
    eyeb = np.eye(128).astype(BF16NP)
    lt, eb, wbc = _moe_consts()
    in_maps = []
    for c in range(NCORES):
        in_maps.append({
            "y3f": y3T, "wg": np.ascontiguousarray(w_gate),
            "wup": np.ascontiguousarray(w_up[c * 8:(c + 1) * 8]),
            "wdn": np.ascontiguousarray(w_down[c * 8:(c + 1) * 8]),
            "const": ones, "lt": lt, "eb": eb, "wb": wbc, "eyeb": eyeb,
            "shift": np.full((1, 1), 8192.0 * c, np.float32),
        })
    res = run_bass_kernel_spmd(nc, in_maps, core_ids=list(range(NCORES)))
    outr = np.concatenate([r["outr"] for r in res.results], axis=0)
    dst6 = res.results[0]["dsto"]
    w6 = res.results[0]["w6o"]
    return outr, dst6, w6



def build_comb():
    nc = bass.Bass("TRN2", target_bir_lowering=False)
    Y3 = nc.dram_tensor("y3", [D, TPC], F32, kind="ExternalInput").ap()
    OUTR = nc.dram_tensor("outr", [65536, D], BF16, kind="ExternalInput").ap()
    DSTI = nc.dram_tensor("dsti", [TPC, 8], F32, kind="ExternalInput").ap()
    W6I = nc.dram_tensor("w6i", [TPC, 8], F32, kind="ExternalInput").ap()
    WSU = nc.dram_tensor("wsu", [D, 1024], F32, kind="ExternalInput").ap()
    WSD = nc.dram_tensor("wsd", [512, D], F32, kind="ExternalInput").ap()
    CONST = nc.dram_tensor("const", [128, 128], F32, kind="ExternalInput").ap()
    EYEF = nc.dram_tensor("eyef", [128, 128], F32, kind="ExternalInput").ap()
    YOUT = nc.dram_tensor("yout", [D, TPC], F32, kind="ExternalOutput").ap()

    p = Prog()
    with ExitStack() as _st:
        ec = _st.enter_context
        ones = ec(nc.sbuf_tensor("ones", [128, 128], F32))
        eye = ec(nc.sbuf_tensor("eye_sb", [128, 128], F32))
        wsu = ec(nc.sbuf_tensor("wsu_sb", [128, 16384], BF16))
        wsd = ec(nc.sbuf_tensor("wsd_sb", [128, 8192], BF16))
        AR1 = ec(nc.sbuf_tensor("AR1", [128, 8192], F32))
        xnb = ec(nc.sbuf_tensor("xnb_t", [128, 8192], BF16))
        ssb = ec(nc.sbuf_tensor("ssb", [128, 4096], F32))
        gsl = ec(nc.sbuf_tensor("gsl", [128, 2048], F32))
        shh = ec(nc.sbuf_tensor("shh", [128, 2048], BF16))
        shsb = ec(nc.sbuf_tensor("shsb", [128, 8192], F32))
        racc = ec(nc.sbuf_tensor("racc", [128, 2048], F32))
        grow = ec(nc.sbuf_tensor("grow", [128, 2048], F32))
        dstf = ec(nc.sbuf_tensor("dstf", [128, 8], F32))
        du = ec(nc.sbuf_tensor("du", [128, 8], U32))
        w6s = ec(nc.sbuf_tensor("w6s", [128, 8], F32))
        lbuf = ec(nc.sbuf_tensor("lbuf", [128, 1024], F32))
        sq = ec(nc.sbuf_tensor("sq", [128, 512], F32))
        ob2 = ec(nc.sbuf_tensor("ob2", [128, 512], F32))
        tmpf = ec(nc.sbuf_tensor("tmpf", [128, 128], F32))
        rows = ec(nc.sbuf_tensor("rows", [1, 2048], F32))
        rows2 = ec(nc.sbuf_tensor("rows2", [1, 2048], F32))
        P0 = ec(nc.psum_tensor("P0", [128, 512], F32))
        P1 = ec(nc.psum_tensor("P1", [128, 512], F32))
        P2 = ec(nc.psum_tensor("P2", [128, 512], F32))
        R0 = ec(nc.psum_tensor("R0", [1, 512], F32))
        R1 = ec(nc.psum_tensor("R1", [1, 512], F32))
        PTF = ec(nc.psum_tensor("PTF", [128, 128], F32))
        S = ec(nc.semaphore("S"))
        blk = ec(nc.Block())

        ones_col = ones[:, 0:1]
        ones_row = ones[0:1, :]
        p.dma(ones[:, :], CONST[:, :])
        p.dma(eye[:, :], EYEF[:, :])
        for c in range(DCH):
            p.dma(lbuf[:, :], WSU[c * 128:(c + 1) * 128, :])
            p.copy(wsu[:, c * 1024:(c + 1) * 1024], lbuf[:, :])
        for j in range(4):
            p.dma(lbuf[:, :], WSD[j * 128:(j + 1) * 128, 0:1024])
            p.copy(wsd[:, j * 2048:j * 2048 + 1024], lbuf[:, :])
            p.dma(lbuf[:, :], WSD[j * 128:(j + 1) * 128, 1024:2048])
            p.copy(wsd[:, j * 2048 + 1024:(j + 1) * 2048], lbuf[:, :])

        xt = AR1
        for ti in range(8):
            t0 = ti * 512
            tw = 512
            for c in range(DCH):
                p.dma(xt[:, c * tw:(c + 1) * tw], Y3[c * 128:(c + 1) * 128, t0:t0 + tw])
            for c in range(DCH):
                cs = slice(c * tw, (c + 1) * tw)
                p.mm(R0[0:1, 0:tw], ones_col, xt[:, cs], start=(c == 0), stop=(c == DCH - 1))
                p.act(sq[:, 0:tw], xt[:, cs], AF.Square)
                p.mm(R1[0:1, 0:tw], ones_col, sq[:, 0:tw], start=(c == 0), stop=(c == DCH - 1))
            m = rows[0:1, 0:tw]
            v = rows2[0:1, 0:tw]
            p.ts(m, R0[0:1, 0:tw], 1.0 / 2048.0, ALU.mult)
            p.ts(v, R1[0:1, 0:tw], 1.0 / 2048.0, ALU.mult)
            mm_r = rows[0:1, 512:512 + tw]
            p.tt(mm_r, m, m, ALU.mult)
            p.tt(v, v, mm_r, ALU.subtract)
            p.ts(v, v, 1e-5, ALU.add)
            p.act(v, v, AF.Ln)
            p.act(v, v, AF.Exp, scale=-0.5)
            p.tt(mm_r, m, v, ALU.mult)
            p.mm(P0[:, 0:tw], ones_row, v, start=True, stop=True)
            p.mm(P2[:, 0:tw], ones_row, mm_r, start=True, stop=True)
            p.scopy(sq[:, 0:tw], P0[:, 0:tw])
            p.scopy(ob2[:, 0:tw], P2[:, 0:tw])
            for c in range(DCH):
                cs = slice(c * tw, (c + 1) * tw)
                p.stt(xnb[:, cs], xt[:, cs], 0.0, sq[:, 0:tw], ALU.add, ALU.mult)
                p.tt(xnb[:, cs], xnb[:, cs], ob2[:, 0:tw], ALU.subtract)
            for hj in range(8):
                for c in range(DCH):
                    p.mm(P0[:, :], wsu[:, c * 1024 + hj * 128: c * 1024 + (hj + 1) * 128],
                         xnb[:, c * 512:(c + 1) * 512], start=(c == 0), stop=(c == DCH - 1))
                p.scopy(ssb[:, hj * 512:(hj + 1) * 512], P0[:, :])
            p.act(gsl[:, :], ssb[:, 2048:4096], AF.Silu)
            p.tt(shh[:, :], gsl[:, :], ssb[:, 0:2048], ALU.mult)
            for fj in range(16):
                for j in range(4):
                    p.mm(P1[:, :], wsd[:, j * 2048 + fj * 128: j * 2048 + (fj + 1) * 128],
                         shh[:, j * 512:(j + 1) * 512], start=(j == 0), stop=(j == 3))
                p.scopy(shsb[:, fj * 512:(fj + 1) * 512], P1[:, :])
            for tq in range(4):
                tq0 = t0 + tq * 128
                p.dma(dstf[:, 0:6], DSTI[tq0:tq0 + 128, 0:6])
                p.dma(w6s[:, 0:6], W6I[tq0:tq0 + 128, 0:6])
                p.copy(du[:, 0:6], dstf[:, 0:6])
                for r in range(6):
                    p.memset(grow[:, :], 0.0)
                    p.idma_gather(grow[:, :], OUTR[:, :], du[:, r:r + 1], 65535)
                    if r == 0:
                        p.ts(racc[:, :], grow[:, :], w6s[:, 0:1], ALU.mult)
                    else:
                        p.stt(racc[:, :], grow[:, :], w6s[:, r:r + 1], racc[:, :],
                              ALU.mult, ALU.add)
                for c in range(DCH):
                    p.tr(PTF[:, :], racc[:, c * 128:(c + 1) * 128], eye[:, :])
                    p.tt(tmpf[:, :], PTF[:, :], shsb[:, c * 512 + tq * 128: c * 512 + (tq + 1) * 128], ALU.add)
                    p.tt(tmpf[:, :], tmpf[:, :], xt[:, c * 512 + tq * 128: c * 512 + (tq + 1) * 128], ALU.add)
                    p.dma(YOUT[c * 128:(c + 1) * 128, tq0:tq0 + 128], tmpf[:, :])

        p.emit(nc, blk, S)
    return nc


def run_comb(y3T_shards, outr, dst6, w6, w_su, w_sd):
    nc = build_comb()
    ones = np.ones((128, 128), np.float32)
    eyef = np.eye(128, dtype=np.float32)
    in_maps = []
    for c in range(NCORES):
        in_maps.append({
            "y3": y3T_shards[c], "outr": outr,
            "dsti": np.ascontiguousarray(dst6[c * TPC:(c + 1) * TPC]),
            "w6i": np.ascontiguousarray(w6[c * TPC:(c + 1) * TPC]),
            "wsu": np.ascontiguousarray(w_su), "wsd": np.ascontiguousarray(w_sd),
            "const": ones, "eyef": eyef,
        })
    res = run_bass_kernel_spmd(nc, in_maps, core_ids=list(range(NCORES)))
    return np.concatenate([r["yout"] for r in res.results], axis=1)
